# version 25
# baseline (speedup 1.0000x reference)
import numpy as np
import ml_dtypes
from contextlib import ExitStack

import concourse.bass as bass
import concourse.mybir as mybir
from concourse.bass_utils import run_bass_kernel_spmd

F32 = mybir.dt.float32
BF16 = mybir.dt.bfloat16
I32 = mybir.dt.int32
AF = mybir.ActivationFunctionType
ALU = mybir.AluOpType
AX = mybir.AxisListType

B, L, D, DI = 2, 4096, 2048, 256
HID = 5632
T = 1024
NKT = D // 128
LN_EPS = 1e-5
Q_EPS = 1e-5
NW1 = float(HID * D)
NW2 = float(D * HID)
CHUNK = 256
NCHUNK = HID // CHUNK
HKT = HID // 128
RG = [list(range(8))]


def build_nc():
    nc = bass.Bass("TRN2", target_bir_lowering=False, debug=False)

    xT = nc.dram_tensor("xT", [D, T], F32, kind="ExternalInput").ap()
    W_in = nc.dram_tensor("W_in", [D, DI], F32, kind="ExternalInput").ap()
    W_f = nc.dram_tensor("W_f", [DI, 2 * DI], F32, kind="ExternalInput").ap()
    W_b = nc.dram_tensor("W_b", [DI, 2 * DI], F32, kind="ExternalInput").ap()
    W_o = nc.dram_tensor("W_o", [2 * DI, D], F32, kind="ExternalInput").ap()
    gamP = nc.dram_tensor("gamP", [128, NKT], F32, kind="ExternalInput").ap()
    betP = nc.dram_tensor("betP", [128, NKT], F32, kind="ExternalInput").ap()
    w1s = nc.dram_tensor("w1s", [D // 8, HID], F32, kind="ExternalInput").ap()
    w3s = nc.dram_tensor("w3s", [D // 8, HID], F32, kind="ExternalInput").ap()
    w2s = nc.dram_tensor("w2s", [HID // 8, D], F32, kind="ExternalInput").ap()
    cmask = nc.dram_tensor("cmask", [128, 64], F32, kind="ExternalInput").ap()
    ones = nc.dram_tensor("ones", [128, 1], F32, kind="ExternalInput").ap()
    identb = nc.dram_tensor("identb", [128, 128], BF16,
                            kind="ExternalInput").ap()
    identf = nc.dram_tensor("identf", [128, 128], F32,
                            kind="ExternalInput").ap()
    onesrow = nc.dram_tensor("onesrow", [1, 128], F32,
                             kind="ExternalInput").ap()

    outT = nc.dram_tensor("outT", [D, T], F32, kind="ExternalOutput").ap()

    ag_sums_in = nc.dram_tensor("ag_sums_in", [1, 4], F32)
    ag_sums_out = nc.dram_tensor("ag_sums_out", [8, 4], F32)
    ag_scan_in = nc.dram_tensor("ag_scan_in", [128, 8], F32)
    ag_scan_out = nc.dram_tensor("ag_scan_out", [1024, 8], F32)
    ag_w1_in = nc.dram_tensor("ag_w1_in", [D // 8, HID], BF16)
    w1g = nc.dram_tensor("w1g", [D, HID], BF16)
    ag_w3_in = nc.dram_tensor("ag_w3_in", [D // 8, HID], BF16)
    w3g = nc.dram_tensor("w3g", [D, HID], BF16)
    ag_w2_in = nc.dram_tensor("ag_w2_in", [HID // 8, D], BF16)
    w2g = nc.dram_tensor("w2g", [HID, D], BF16)
    rowb = nc.dram_tensor("rowb", [1, T], F32)

    es = ExitStack()
    with es:
        XB = es.enter_context(nc.sbuf_tensor("XB", [128, NKT * T], F32))
        RA = es.enter_context(nc.sbuf_tensor("RA", [128, 22528], F32))
        QN = es.enter_context(nc.sbuf_tensor("QN", [128, 8192], BF16))
        SH = es.enter_context(nc.sbuf_tensor("SH", [128, 8192], F32))
        SCR = es.enter_context(nc.sbuf_tensor("SCR", [128, 512], F32))
        SCI = es.enter_context(nc.sbuf_tensor("SCI", [128, 512], I32))
        COLS = es.enter_context(nc.sbuf_tensor("COLS", [128, 64], F32))
        GATH = es.enter_context(nc.sbuf_tensor("GATH", [128, 80], F32))
        MASK = es.enter_context(nc.sbuf_tensor("MASK", [128, 64], F32))
        SMALL = es.enter_context(nc.sbuf_tensor("SMALL", [128, 32], F32))
        ONES = es.enter_context(nc.sbuf_tensor("ONES", [128, 1], F32))
        IDB = es.enter_context(nc.sbuf_tensor("IDB", [128, 128], BF16))
        IDF = es.enter_context(nc.sbuf_tensor("IDF", [128, 128], F32))
        ONESR = es.enter_context(nc.sbuf_tensor("ONESR", [1, 128], F32))
        PS = [es.enter_context(nc.psum_tensor(f"PS{i}", [128, 512], F32))
              for i in range(8)]
        sem = {n: es.enter_context(nc.semaphore(n))
               for n in ["dA", "dG", "dS", "TS", "VS", "SS", "GS", "CC",
                         "dW0", "dW1", "dT0", "dT1", "dL0", "dL1", "dX",
                         "dV0", "dV1", "dU0", "dU1", "dE"]}

        steps = {"sync": [], "vector": [], "scalar": [], "gpsimd": [],
                 "tensor": []}
        cnt = {}

        def bump(nm, amt=1):
            cnt[nm] = cnt.get(nm, 0) + amt
            return (nm, cnt[nm])

        QSEM = {"sync": "dA", "gpsimd": "dG", "scalar": "dS",
                "vector": "dE"}
        ESEM = {"vector": "VS", "scalar": "SS", "tensor": "TS",
                "gpsimd": "GS"}

        def dma(queue, out, in_, after=(), slow=False, semname=None):
            tok = bump(semname or QSEM[queue], 16)

            def fn(eng, out=out, in_=in_, after=tuple(after), tok=tok):
                for nm, v in after:
                    eng.wait_ge(sem[nm], v)
                eng.dma_start(out=out, in_=in_,
                              allow_slow_non_contiguous=slow
                              ).then_inc(sem[tok[0]], 16)
            steps[queue].append(fn)
            return tok

        def op(engine, fn, after=()):
            tok = bump(ESEM[engine])

            def wrapped(eng, fn=fn, after=tuple(after), tok=tok):
                for nm, v in after:
                    eng.wait_ge(sem[nm], v)
                fn(eng).then_inc(sem[tok[0]], 1)
            steps[engine].append(wrapped)
            return tok

        def cc(ins_h, outs_h, after=()):
            tok = bump("CC")

            def fn(eng, tok=tok, after=tuple(after)):
                for nm, v in after:
                    eng.wait_ge(sem[nm], v)
                eng.collective_compute(
                    "AllGather", ALU.bypass, replica_groups=RG,
                    ins=[ins_h.ap().opt()], outs=[outs_h.ap().opt()]
                ).then_inc(sem["CC"], 1)
            steps["gpsimd"].append(fn)
            return tok

        ra, shp, xb, qn = RA.ap(), SH.ap(), XB.ap(), QN.ap()
        colv, smv, gat, msk = COLS.ap(), SMALL.ap(), GATH.ap(), MASK.ap()
        scr, sci = SCR.ap(), SCI.ap()

        def raf(a, n):
            return ra[:, a:a + n]

        def rabf(a, n):
            return ra[:, a:a + (n + 1) // 2].bitcast(BF16)[:, 0:n]

        def rai32(a, n):
            return ra[:, a:a + n].bitcast(I32)[:, 0:n]

        def slot(i):
            return shp[:, i * T:(i + 1) * T]

        def srow(i, a=0, n=T):
            return shp[0:1, i * T + a:i * T + a + n]

        def sbuf_stream(i):
            return shp[:, 4096 + i * 2048:4096 + (i + 1) * 2048].bitcast(
                BF16)[:, 0:4096]

        def xtile(kt, c0=0, cn=T):
            return xb[:, kt * T + c0:kt * T + c0 + cn]

        O_WIN, O_WFB, O_Z, O_ZR, O_DG, O_P, O_H = (
            0, 4096, 6144, 8192, 10240, 14336, 18432)
        O_Y = 8192
        O_HR = 16384
        qnf = qn[:, 0:8192].bitcast(F32)

        def wqa(i):
            return qnf[:, i * 1024:(i + 1) * 1024]

        wqf2 = qnf[:, 2048:3072]
        wqf3 = qnf[:, 3072:4096]

        def wqb(i):
            return shp[:, 6144 + i * 512:6144 + (i + 1) * 512].bitcast(
                BF16)[:, 0:1024]

        dma("sync", ONES[:], ones)
        dma("sync", colv[:, 0:NKT], gamP)
        dma("sync", colv[:, NKT:2 * NKT], betP)
        dma("sync", msk[:, :], cmask)
        dma("sync", IDB[:], identb)
        dma("sync", IDF[:], identf)
        dma("sync", ONESR[:], onesrow)
        for kt in range(8):
            dma("sync", xtile(kt), xT[kt * 128:(kt + 1) * 128, :])
        tBAT1 = ("dA", cnt["dA"])
        for kt in range(8, NKT):
            dma("scalar", xtile(kt), xT[kt * 128:(kt + 1) * 128, :],
                semname="dE")
        tBATX = ("dE", cnt["dE"])
        dma("sync", raf(O_WIN, 4096).rearrange("p (k c) -> p k c", k=16),
            W_in.rearrange("(k p) c -> p k c", p=128), after=[tBAT1])
        dma("sync", raf(O_WFB, 1024).rearrange("p (k c) -> p k c", k=2),
            W_f.rearrange("(k p) c -> p k c", p=128))
        dma("sync", raf(O_WFB + 1024, 1024).rearrange("p (k c) -> p k c", k=2),
            W_b.rearrange("(k p) c -> p k c", p=128))
        tINIT2 = ("dA", cnt["dA"])
        t_x = [tBAT1, tBATX]
        t_ones = t_gam = t_bet = t_mask = t_idb = t_idf = t_onr = tBAT1
        t_win = t_wf = t_wb = tINIT2

        def wchunks(ncols):
            out, a = [], 0
            while a < ncols:
                n = min(1024, ncols - a)
                out.append((a, n))
                a += n
            return out

        wq_buf_free = {}

        def wsum_pass(wap, nrows, ncols, acc_col, queue, semn, bufn, bufs,
                      prev=()):
            acc = smv[:, acc_col:acc_col + 1]
            tred = op("vector", lambda eng: eng.memset(acc, 0.0),
                      after=list(prev))
            ci = 0
            for pt in range((nrows + 127) // 128):
                pr = min(128, nrows - pt * 128)
                for (a, n) in wchunks(ncols):
                    bi = ci % 2
                    key = bufn + str(bi)
                    dep = ([wq_buf_free[key]] if wq_buf_free.get(key)
                           else [])
                    tl = dma(queue, bufs[bi][0:pr, 0:n],
                             wap[pt * 128:pt * 128 + pr, a:a + n], after=dep,
                             semname="%s%d" % (semn, bi))
                    tr = op("vector",
                            lambda eng, pr=pr, n=n, bi=bi, bufs=bufs:
                            eng.reduce_sum(
                                scr[0:pr, 0:1], bufs[bi][0:pr, 0:n], AX.X,
                                apply_absolute_value=True),
                            after=[tl, tred])
                    tred = op("vector",
                              lambda eng, pr=pr: eng.tensor_tensor(
                                  acc[0:pr, :], acc[0:pr, :],
                                  scr[0:pr, 0:1], ALU.add),
                              after=[tr, tred])
                    wq_buf_free[key] = tr
                    ci += 1
            return tred

        shbuf = [shp[:, 4096:5120], shp[:, 5120:6144]]
        t_r1_ = wsum_pass(w1s, 256, HID, 0, "gpsimd", "dW", "W",
                          [wqa(0), wqa(1)])
        t_r3_ = wsum_pass(w3s, 256, HID, 1, "sync", "dV", "V",
                          [wqa(2), wqa(3)], prev=[t_r1_])
        t_r2_ = wsum_pass(w2s, 704, D, 2, "scalar", "dU", "U", shbuf,
                          prev=[t_r3_])

        def ln_stats(src, ncols, sq_scratch, mu_row, r_row, scratch_row,
                     after=()):
            nnc = ncols // 512
            after = list(after)
            tmm = op("tensor",
                     lambda eng: [
                         eng.matmul(PS[4 + nc_][0:1, :], ONES[:],
                                    src(kt)[:, nc_ * 512:(nc_ + 1) * 512],
                                    start=(kt == 0), stop=(kt == NKT - 1))
                         for nc_ in range(nnc) for kt in range(NKT)][-1],
                     after=after + [t_ones])
            tq_mm = None
            for kt in range(NKT):
                tS = op("scalar",
                        lambda eng, kt=kt: eng.activation(
                            sq_scratch, src(kt), AF.Square),
                        after=after + ([tq_mm] if tq_mm else []))
                tq_mm = op("tensor",
                           lambda eng, kt=kt: [
                               eng.matmul(
                                   PS[6 + nc_][0:1, :], ONES[:],
                                   sq_scratch[:, nc_ * 512:(nc_ + 1) * 512],
                                   start=(kt == 0), stop=(kt == NKT - 1))
                               for nc_ in range(nnc)][-1],
                           after=[tS])
            tmu = []
            tms = []
            for nc_ in range(nnc):
                tmu.append(op(
                    "vector",
                    lambda eng, nc_=nc_: eng.tensor_scalar(
                        mu_row[0:1, nc_ * 512:(nc_ + 1) * 512],
                        PS[4 + nc_][0:1, :], 1.0 / D, None, ALU.mult),
                    after=[tmm]))
                tms.append(op(
                    "vector",
                    lambda eng, nc_=nc_: eng.tensor_scalar(
                        scratch_row[0:1, nc_ * 512:(nc_ + 1) * 512],
                        PS[6 + nc_][0:1, :], 1.0 / D, None, ALU.mult),
                    after=[tq_mm]))
            tv1 = op("vector",
                     lambda eng: eng.tensor_tensor(
                         r_row, mu_row, mu_row, ALU.mult), after=tmu)
            tv2 = op("vector",
                     lambda eng: eng.tensor_tensor(
                         scratch_row, scratch_row, r_row, ALU.subtract),
                     after=tms + [tv1])
            tv3 = op("vector",
                     lambda eng: eng.tensor_single_scalar(
                         scratch_row, scratch_row, LN_EPS, ALU.add),
                     after=[tv2])
            tl1 = op("scalar",
                     lambda eng: eng.activation(r_row, scratch_row, AF.Ln),
                     after=[tv3])
            tl2 = op("scalar",
                     lambda eng: eng.activation(r_row, r_row, AF.Exp,
                                                scale=-0.5), after=[tl1])
            tfree = tmu + tms
            return tl2, tfree

        t_r1, t_ln1free = ln_stats(lambda kt: xtile(kt), 1024, slot(2),
                                   srow(0), srow(1), srow(2), after=t_x)
        t_mur1 = op("vector",
                    lambda eng: eng.tensor_tensor(
                        srow(2), srow(0), srow(1), ALU.mult), after=[t_r1])
        def bcast_row(row_ap, dst_ap, n, bank, after):
            tmm = op("tensor",
                     lambda eng: eng.matmul(PS[bank][:, 0:n], ONESR[:],
                                            row_ap, start=True, stop=True),
                     after=list(after) + [t_onr])
            return op("scalar",
                      lambda eng: eng.copy(dst_ap, PS[bank][:, 0:n]),
                      after=[tmm])

        tb0 = bcast_row(srow(1, 0, 512), slot(0)[:, 0:512], 512, 2,
                        [t_r1, t_mur1])
        t_r1b = bcast_row(srow(1, 512, 512), slot(0)[:, 512:1024], 512, 3,
                          [t_r1, tb0])
        tb1 = bcast_row(srow(2, 0, 512), slot(1)[:, 0:512], 512, 2,
                        [t_mur1, t_r1b])
        t_mur1b = bcast_row(srow(2, 512, 512), slot(1)[:, 512:1024], 512, 3,
                            [t_mur1, tb1])

        def u1t(i):
            return raf(O_DG + i * 1024, 1024)

        t_u1_mm = {}
        for kt in range(NKT):
            buf = kt % 2
            dep = [t_r1b, t_mur1b]
            if kt >= 2:
                dep.append(t_u1_mm[kt - 2])
            ta = op("vector",
                    lambda eng, kt=kt, buf=buf: eng.tensor_tensor(
                        u1t(buf), xtile(kt), slot(0), ALU.mult), after=dep)
            tb = op("vector",
                    lambda eng, buf=buf: eng.tensor_tensor(
                        u1t(buf), u1t(buf), slot(1), ALU.subtract),
                    after=[ta])
            t_u1_mm[kt] = op(
                "tensor",
                lambda eng, kt=kt, buf=buf: [
                    eng.matmul(PS[mt * 2 + nc_][:, :],
                               raf(O_WIN + kt * 256 + mt * 128, 128),
                               u1t(buf)[:, nc_ * 512:(nc_ + 1) * 512],
                               start=(kt == 0), stop=(kt == NKT - 1))
                    for mt in range(2) for nc_ in range(2)][-1],
                after=[tb, t_win] + ([] if kt else [t_r1b, t_mur1b]))
        t_zev = []
        for mt in range(2):
            for nc_ in range(2):
                t_zev.append(op(
                    "scalar",
                    lambda eng, mt=mt, nc_=nc_: eng.copy(
                        raf(O_Z + mt * 1024 + nc_ * 512, 512),
                        PS[mt * 2 + nc_][:, :]),
                    after=[t_u1_mm[NKT - 1]]))

        t_zrev = []
        for g in range(2):
            t_zrev.append(op(
                "vector",
                lambda eng, g=g: eng.tensor_copy(
                    raf(O_ZR + g * 1024, 1024),
                    raf(O_Z + g * 1024, 1024)[:, ::-1]),
                after=t_zev))

        t_zero = op("vector",
                    lambda eng: eng.memset(raf(O_WIN + 1024, 1024), 0.0),
                    after=[t_u1_mm[NKT - 1]])
        zeros1024 = raf(O_WIN + 1024, 1024)

        t_smm = op("tensor",
                   lambda eng: eng.matmul(PS[7][0:1, 0:3], ONES[:],
                                          smv[:, 0:3], start=True, stop=True),
                   after=[t_r1_, t_r3_, t_r2_, t_ones] + t_ln1free)
        t_s2 = op("scalar",
                  lambda eng: eng.copy(smv[0:1, 0:3], PS[7][0:1, 0:3]),
                  after=[t_smm])
        t_z3 = op("vector", lambda eng: eng.memset(smv[0:1, 3:4], 0.0))
        t_dsum = dma("gpsimd", ag_sums_in[0:1, 0:4], smv[0:1, 0:4],
                     after=[t_s2, t_z3])
        t_ccs = cc(ag_sums_in, ag_sums_out, after=[t_dsum])
        t_dsb = dma("gpsimd", smv[0:8, 8:12], ag_sums_out[:, :],
                    after=[t_ccs])
        t_tot = op("tensor",
                   lambda eng: eng.matmul(PS[6][0:1, 0:4], ONES[0:8, :],
                                          smv[0:8, 8:12], start=True,
                                          stop=True),
                   after=[t_dsb, t_ones] + t_ln1free)
        t_totr = op("scalar",
                    lambda eng: eng.copy(smv[0:1, 8:12], PS[6][0:1, 0:4]),
                    after=[t_tot])
        t_bcm = op("tensor",
                   lambda eng: eng.matmul(PS[6][:, 0:4], ONESR[:],
                                          smv[0:1, 8:12], start=True,
                                          stop=True),
                   after=[t_totr, t_onr])
        t_t128 = op("scalar",
                    lambda eng: eng.copy(smv[:, 12:16], PS[6][:, 0:4]),
                    after=[t_bcm])
        t_mwa = op("vector",
                   lambda eng: eng.tensor_scalar(
                       smv[:, 16:17], smv[:, 12:13], 1.0 / NW1, Q_EPS,
                       ALU.mult, ALU.max), after=[t_t128])
        t_mwb = op("vector",
                   lambda eng: eng.tensor_scalar(
                       smv[:, 17:18], smv[:, 13:14], 1.0 / NW1, Q_EPS,
                       ALU.mult, ALU.max), after=[t_t128])
        t_mwc = op("vector",
                   lambda eng: eng.tensor_scalar(
                       smv[:, 18:19], smv[:, 14:15], 1.0 / NW2, Q_EPS,
                       ALU.mult, ALU.max), after=[t_t128])
        t_wsc = op("vector",
                   lambda eng: eng.tensor_single_scalar(
                       smv[:, 20:23], smv[:, 16:19], 0.5, ALU.mult),
                   after=[t_mwa, t_mwb, t_mwc])

        wqb_free = {}

        def wquant_pass(wap, nrows, ncols, scol, ag_in_h, ag_out_h,
                        defer_cc=False):
            tdma_last = None
            tq3_prev = None
            ci = 0
            for pt in range((nrows + 127) // 128):
                pr = min(128, nrows - pt * 128)
                for (a, n) in wchunks(ncols):
                    bi = ci % 2
                    key = "W%d" % bi
                    dep = [t_wsc] + ([wq_buf_free[key]]
                                     if wq_buf_free.get(key) else [])
                    tl = dma("sync" if bi == 0 else "scalar",
                             wqa(bi)[0:pr, 0:n],
                             wap[pt * 128:pt * 128 + pr, a:a + n], after=dep,
                             semname="dW%d" % bi)
                    tq1 = op("scalar",
                             lambda eng, pr=pr, n=n, bi=bi:
                             eng.activation(
                                 wqf2[0:pr, 0:n], wqa(bi)[0:pr, 0:n],
                                 AF.Abs),
                             after=[tl, t_wsc] +
                             ([tq3_prev] if tq3_prev else []))
                    tq2 = op("scalar",
                             lambda eng, pr=pr, n=n, bi=bi:
                             eng.activation(
                                 wqf3[0:pr, 0:n], wqa(bi)[0:pr, 0:n],
                                 AF.Sign), after=[tq1])
                    tq2v = op("vector",
                              lambda eng, pr=pr, n=n, scol=scol:
                              eng.tensor_scalar(
                                  wqf2[0:pr, 0:n], wqf2[0:pr, 0:n],
                                  smv[0:pr, 20 + scol:21 + scol], None,
                                  ALU.is_gt),
                              after=[tq1] +
                              ([tq3_prev] if tq3_prev else []))
                    tq3 = op("vector",
                             lambda eng, pr=pr, n=n, bi=bi: eng.tensor_tensor(
                                 wqb(bi)[0:pr, 0:n], wqf2[0:pr, 0:n],
                                 wqf3[0:pr, 0:n], ALU.mult),
                             after=[tq2v, tq2] +
                             ([wqb_free[bi]] if wqb_free.get(bi) else []))
                    tdma_last = dma("scalar",
                                    ag_in_h[pt * 128:pt * 128 + pr, a:a + n],
                                    wqb(bi)[0:pr, 0:n], after=[tq3],
                                    semname="dT%d" % bi)
                    wq_buf_free["W%d" % bi] = tq2
                    wqb_free[bi] = tdma_last
                    tq3_prev = tq3
                    ci += 1
            tboth = [("dT0", cnt.get("dT0", 0)), ("dT1", cnt.get("dT1", 0))]
            if defer_cc:
                return tboth, tboth[0]
            return cc(ag_in_h, ag_out_h, after=tboth), tboth[0]



        t_scan_done = []
        t_extr = []
        prev_dg_ev = []
        for dire in range(2):
            zin = raf(O_Z, 2048) if dire == 0 else raf(O_ZR, 2048)
            wofs = O_WFB + dire * 1024
            tw = t_wf if dire == 0 else t_wb
            t_dg = []
            for nc_ in range(2):
                dep_mm = [tw] + t_zev + t_zrev + prev_dg_ev + t_dg
                tmm = op(
                    "tensor",
                    lambda eng, nc_=nc_, zin=zin, wofs=wofs: [
                        eng.matmul(PS[mt][:, :],
                                   raf(wofs + kt * 512 + mt * 128, 128),
                                   zin[:, kt * 1024 + nc_ * 512:
                                       kt * 1024 + (nc_ + 1) * 512],
                                   start=(kt == 0), stop=(kt == 1))
                        for mt in range(4) for kt in range(2)][-1],
                    after=dep_mm)
                for mt in range(2):
                    t_dg.append(op(
                        "scalar",
                        lambda eng, mt=mt, nc_=nc_: eng.activation(
                            raf(O_DG + mt * 1024 + nc_ * 512, 512),
                            PS[mt][:, :], AF.Sigmoid),
                        after=[tmm] + t_scan_done))
                for mt in range(2, 4):
                    t_dg.append(op(
                        "vector",
                        lambda eng, mt=mt, nc_=nc_, zin=zin:
                        eng.tensor_tensor(
                            raf(O_DG + 2048 + (mt - 2) * 1024 + nc_ * 512,
                                512),
                            PS[mt][:, :],
                            zin[:, (mt - 2) * 1024 + nc_ * 512:
                                (mt - 2) * 1024 + (nc_ + 1) * 512],
                            ALU.mult),
                        after=[tmm] + t_scan_done))
            prev_dg_ev = list(t_dg)
            t_sc = []
            for g in range(2):
                extra = []
                t_sc.append(op(
                    "vector",
                    lambda eng, g=g, dire=dire: eng.tensor_tensor_scan(
                        raf(O_H + dire * 2048 + g * 1024, 1024),
                        raf(O_DG + g * 1024, 1024),
                        raf(O_DG + 2048 + g * 1024, 1024),
                        0.0, ALU.mult, ALU.add),
                    after=t_dg + extra))
                t_sc.append(op(
                    "vector",
                    lambda eng, g=g, dire=dire: eng.tensor_tensor_scan(
                        raf(O_P + dire * 2048 + g * 1024, 1024),
                        raf(O_DG + g * 1024, 1024),
                        zeros1024, 1.0, ALU.mult, ALU.add),
                    after=t_dg + [t_zero] + extra))
            for g in range(2):
                t_extr.append(op(
                    "vector",
                    lambda eng, g=g, dire=dire: eng.tensor_copy(
                        gat[:, dire * 4 + 2 * g:dire * 4 + 2 * g + 1],
                        raf(O_P + dire * 2048 + g * 1024 + 1023, 1)),
                    after=t_sc))
                t_extr.append(op(
                    "vector",
                    lambda eng, g=g, dire=dire: eng.tensor_copy(
                        gat[:, dire * 4 + 2 * g + 1:dire * 4 + 2 * g + 2],
                        raf(O_H + dire * 2048 + g * 1024 + 1023, 1)),
                    after=t_sc))
            t_scan_done = t_scan_done + t_sc

        t_dsc = dma("gpsimd", ag_scan_in[:, :], gat[:, 0:8], after=t_extr,
                    semname="dX")
        t_ccscan = cc(ag_scan_in, ag_scan_out, after=[t_dsc])
        t_dscb = dma("gpsimd", gat[:, 16:80].rearrange("p (c q) -> p c q", c=8),
                     ag_scan_out.ap().rearrange("(c p) q -> p c q", p=128),
                     after=[t_ccscan], semname="dX")

        t_J = {}
        for dire in range(2):
            J = smv[:, 26 + dire * 2:28 + dire * 2]
            A = gat[:, 8:10]
            Ssel = gat[:, 10:12]
            tj = op("vector", lambda eng, J=J: eng.memset(J, 0.0),
                    after=[t_dscb, t_wsc] + ([t_J[0]] if dire else []))
            mbase = dire * 27
            for s in range(3):
                ta = op("vector",
                        lambda eng, s=s, mbase=mbase, A=A: eng.tensor_copy(
                            A[:, 0:1],
                            msk[:, mbase + 24 + s:mbase + 25 + s]),
                        after=[tj, t_mask])
                ta = op("vector",
                        lambda eng, s=s, mbase=mbase, A=A: eng.tensor_copy(
                            A[:, 1:2],
                            msk[:, mbase + 24 + s:mbase + 25 + s]),
                        after=[ta])
                ts_ = op("vector",
                         lambda eng, Ssel=Ssel: eng.memset(Ssel, 0.0),
                         after=[tj])
                for j in range(8):
                    q0 = 16 + j * 8 + dire * 4
                    ta = op("vector",
                            lambda eng, q0=q0, s=s, j=j, mbase=mbase, A=A:
                            eng.scalar_tensor_tensor(
                                A, gat[:, q0:q0 + 3:2],
                                msk[:, mbase + s * 8 + j:
                                    mbase + s * 8 + j + 1],
                                A, ALU.mult, ALU.add), after=[ta])
                    ts_ = op("vector",
                             lambda eng, q0=q0, s=s, j=j, mbase=mbase,
                             Ssel=Ssel: eng.scalar_tensor_tensor(
                                 Ssel, gat[:, q0 + 1:q0 + 4:2],
                                 msk[:, mbase + s * 8 + j:
                                     mbase + s * 8 + j + 1],
                                 Ssel, ALU.mult, ALU.add), after=[ts_])
                tj = op("vector",
                        lambda eng, J=J, A=A: eng.tensor_tensor(
                            J, J, A, ALU.mult), after=[ta, tj])
                tj = op("vector",
                        lambda eng, J=J, Ssel=Ssel: eng.tensor_tensor(
                            J, J, Ssel, ALU.add), after=[ts_, tj])
            t_J[dire] = tj

        t_hfix = []
        for dire in range(2):
            for g in range(2):
                t_hfix.append(op(
                    "vector",
                    lambda eng, dire=dire, g=g: eng.scalar_tensor_tensor(
                        raf(O_H + dire * 2048 + g * 1024, 1024),
                        raf(O_P + dire * 2048 + g * 1024, 1024),
                        smv[:, 26 + dire * 2 + g:27 + dire * 2 + g],
                        raf(O_H + dire * 2048 + g * 1024, 1024),
                        ALU.mult, ALU.add),
                    after=[t_J[dire]] + t_scan_done))
        t_hrev = []
        for g in range(2):
            t_hrev.append(op(
                "vector",
                lambda eng, g=g: eng.tensor_copy(
                    raf(O_HR + g * 1024, 1024),
                    raf(O_H + 2048 + g * 1024, 1024)[:, ::-1]),
                after=t_hfix))

        def hgrp(kt):
            return raf(O_H + kt * 1024, 1024) if kt < 2 else \
                raf(O_HR + (kt - 2) * 1024, 1024)

        t_wo = dma("gpsimd",
                   raf(O_WIN, 8192).rearrange("p (k c) -> p k c", k=4),
                   W_o.rearrange("(k p) c -> p k c", p=128),
                   after=t_scan_done + t_zrev + [t_t128])

        def make_yprep(hf, extra):
            c0 = hf * 512
            vp, yp = None, {}
            for mtd in range(NKT):
                depv = [t_r1b, t_mur1b] + (t_hfix if mtd >= 12
                                           else t_scan_done) + list(extra)
                if vp:
                    depv.append(vp)
                v1 = op("vector",
                        lambda eng, mtd=mtd, c0=c0: eng.tensor_tensor(
                            raf(O_Y + mtd * 512, 512), xtile(mtd, c0, 512),
                            slot(0)[:, c0:c0 + 512], ALU.mult), after=depv)
                v2 = op("vector",
                        lambda eng, mtd=mtd, c0=c0: eng.tensor_tensor(
                            raf(O_Y + mtd * 512, 512),
                            raf(O_Y + mtd * 512, 512),
                            slot(1)[:, c0:c0 + 512], ALU.subtract),
                        after=[v1])
                yp[mtd] = v2
                vp = v2
            return yp

        yprep0 = make_yprep(0, [])

        w1deps, _ = wquant_pass(w1s, 256, HID, 0, ag_w1_in, w1g,
                                defer_cc=True)
        w3deps, _ = wquant_pass(w3s, 256, HID, 1, ag_w3_in, w3g,
                                defer_cc=True)
        t_agw1 = cc(ag_w1_in, w1g, after=w1deps)
        t_agw3 = cc(ag_w3_in, w3g, after=w3deps)

        t_x1 = {}
        prev_bank_ev = {0: None, 1: None}
        ps5_prev = t_ln1free
        scr_prev = [t_wsc]
        for hf in range(2):
            c0 = hf * 512
            t_yprep = yprep0 if hf == 0 else make_yprep(1, [t_x1[0]])
            t_yev = []
            for mtd in range(NKT):
                bank = mtd % 2
                dep = [t_wo, t_idf, t_yprep[mtd]] + t_hrev + t_hfix
                if prev_bank_ev[bank] is not None:
                    dep.append(prev_bank_ev[bank])
                tmm = op(
                    "tensor",
                    lambda eng, mtd=mtd, bank=bank, c0=c0: ([
                        eng.matmul(PS[bank][:, :],
                                   raf(O_WIN + kt * 2048 + mtd * 128, 128),
                                   hgrp(kt)[:, c0:c0 + 512],
                                   start=(kt == 0), stop=False)
                        for kt in range(4)] and
                        eng.matmul(PS[bank][:, :], IDF[:],
                                   raf(O_Y + mtd * 512, 512),
                                   start=False, stop=True)),
                    after=dep)
                te3 = op("scalar",
                         lambda eng, mtd=mtd, bank=bank: eng.copy(
                             raf(O_Y + mtd * 512, 512), PS[bank][:, :]),
                         after=[tmm])
                prev_bank_ev[bank] = te3
                t_yev.append(te3)

            def ysrc(kt):
                return raf(O_Y + kt * 512, 512)

            t_rby, t_lnbfree = ln_stats(
                ysrc, 512, slot(2)[:, 0:512], scr[0:1, 0:512],
                srow(2, 0, 512), srow(3, 0, 512),
                after=t_yev + scr_prev + t_ln1free)
            scr_prev = [t_lnbfree[0]]
            t_mury = op("vector",
                        lambda eng: eng.tensor_tensor(
                            scr[0:1, 0:512], scr[0:1, 0:512],
                            srow(2, 0, 512), ALU.mult), after=[t_rby])
            t_ryb = bcast_row(srow(2, 0, 512), slot(3)[:, 0:512], 512,
                              5, [t_rby] + ps5_prev)
            t_muryb = bcast_row(scr[0:1, 0:512], slot(2)[:, 0:512], 512,
                                5, [t_mury, t_ryb])
            ps5_prev = [t_muryb]
            tprev = t_muryb
            for kt in range(NKT):
                ta = op("vector",
                        lambda eng, kt=kt: eng.tensor_tensor(
                            scr[:, :], raf(O_Y + kt * 512, 512),
                            slot(3)[:, 0:512], ALU.mult),
                        after=[t_ryb, t_muryb, t_gam, t_bet, tprev,
                               t_mury])
                tb = op("vector",
                        lambda eng: eng.tensor_tensor(
                            scr[:, :], scr[:, :], slot(2)[:, 0:512],
                            ALU.subtract), after=[ta])
                tc_ = op("vector",
                         lambda eng, kt=kt, c0=c0: eng.scalar_tensor_tensor(
                             xtile(kt, c0, 512), scr[:, :],
                             colv[:, kt:kt + 1], xtile(kt, c0, 512),
                             ALU.mult, ALU.add), after=[tb])
                td = op("vector",
                        lambda eng, kt=kt, c0=c0: eng.tensor_scalar(
                            xtile(kt, c0, 512), xtile(kt, c0, 512),
                            colv[:, NKT + kt:NKT + kt + 1], None, ALU.add),
                        after=[tc_])
                tprev = td
            scr_prev = [tprev]
            t_x1[hf] = tprev

        t_agw2, t_wqdma2 = wquant_pass(w2s, 704, D, 2, ag_w2_in, w2g)
        t_wqdma2b = ("dT1", cnt.get("dT1", 0))

        t_half_done = {}
        for hf in range(2):
            c0 = hf * 512
            dep0 = [t_x1[0], t_x1[1]]
            if hf == 1:
                dep0 = dep0 + [t_half_done[0]]

            def x1src(kt, c0=c0):
                return xtile(kt, c0, 512)

            t_r2, t_lncfree = ln_stats(
                x1src, 512, slot(2)[:, 0:512], scr[0:1, 0:512],
                srow(0, 0, 512), srow(1, 0, 512), after=dep0 + scr_prev)
            t_mu2b = bcast_row(scr[0:1, 0:512], slot(1)[:, 0:512], 512,
                               5, [t_r2] + ps5_prev)
            ps5_prev = [t_mu2b]
            tm = op("vector",
                    lambda eng: eng.memset(slot(3)[:, 0:512], 0.0),
                    after=[t_mu2b] + t_lncfree + dep0)
            for kt in range(NKT):
                ta = op("vector",
                        lambda eng, kt=kt, c0=c0: eng.tensor_tensor(
                            slot(2)[:, 0:512], xtile(kt, c0, 512),
                            slot(1)[:, 0:512], ALU.subtract),
                        after=[t_mu2b, tm] + t_lncfree)
                tm = op("vector",
                        lambda eng: eng.tensor_tensor(
                            slot(3)[:, 0:512], slot(3)[:, 0:512],
                            slot(2)[:, 0:512], ALU.max), after=[ta])
                tm = op("vector",
                        lambda eng: eng.scalar_tensor_tensor(
                            slot(3)[:, 0:512], slot(2)[:, 0:512], -1.0,
                            slot(3)[:, 0:512], ALU.mult, ALU.max),
                        after=[tm])
            tmx = tm
            for blk in range(4):
                tpm = op("tensor",
                         lambda eng, blk=blk: eng.transpose(
                             PS[5][:, 0:128],
                             slot(3)[:, blk * 128:(blk + 1) * 128],
                             IDF[:]),
                         after=[tmx, t_idf] + ps5_prev)
                tmx = op("vector",
                         lambda eng, blk=blk: eng.reduce_max(
                             colv[:, 60 + blk:61 + blk], PS[5][:, 0:128],
                             AX.X), after=[tpm])
                ps5_prev = [tmx]
            t_mxo = dma("gpsimd",
                        rowb.ap()[0:1, 0:512].rearrange(
                            "a (m p) -> (a p) m", p=128).opt(),
                        colv[:, 60:64], after=[tmx], slow=True)
            t_mxi = dma("gpsimd", srow(1, 512, 512), rowb[0:1, 0:512],
                        after=[t_mxo])
            t1 = op("vector",
                    lambda eng: eng.tensor_tensor(
                        srow(1, 512, 512), srow(1, 512, 512),
                        srow(0, 0, 512), ALU.mult), after=[t_mxi, t_r2])
            t2 = op("vector",
                    lambda eng: eng.tensor_single_scalar(
                        srow(1, 512, 512), srow(1, 512, 512), Q_EPS,
                        ALU.max), after=[t1])
            t3 = op("vector",
                    lambda eng: eng.reciprocal(
                        srow(2, 0, 512), srow(1, 512, 512)), after=[t2])
            t4 = op("vector",
                    lambda eng: eng.tensor_single_scalar(
                        srow(2, 0, 512), srow(2, 0, 512), 127.0, ALU.mult),
                    after=[t3])
            t5 = op("vector",
                    lambda eng: eng.tensor_single_scalar(
                        srow(2, 512, 512), srow(1, 512, 512), 1.0 / 127.0,
                        ALU.mult), after=[t4])
            tk1 = op("vector",
                     lambda eng: eng.tensor_scalar(
                         srow(3, 0, 512), srow(2, 512, 512),
                         smv[0:1, 16:17], None, ALU.mult), after=[t5])
            tk3 = op("vector",
                     lambda eng: eng.tensor_scalar(
                         srow(3, 512, 512), srow(2, 512, 512),
                         smv[0:1, 17:18], None, ALU.mult), after=[t5])
            trs = op("vector",
                     lambda eng: eng.tensor_tensor(
                         srow(0, 0, 512), srow(0, 0, 512), srow(2, 0, 512),
                         ALU.mult), after=[t4, tk1, tk3, t1])
            tc1 = op("vector",
                     lambda eng: eng.tensor_tensor(
                         scr[0:1, 0:512], scr[0:1, 0:512], srow(0, 0, 512),
                         ALU.mult), after=[trs, t_mu2b])
            tc2 = op("vector",
                     lambda eng: eng.tensor_scalar(
                         scr[0:1, 0:512], scr[0:1, 0:512], -1.0, 128.0,
                         ALU.mult, ALU.add), after=[tc1])
            t_rsb = bcast_row(srow(0, 0, 512), slot(1)[:, 0:512], 512,
                              5, [trs, t_mu2b] + ps5_prev)
            t_cb = bcast_row(scr[0:1, 0:512], slot(0)[:, 0:512], 512,
                             5, [tc2, t_rsb])
            ps5_prev = [t_cb]
            t_k1o = dma("gpsimd", rowb[0:1, 0:512], srow(3, 0, 512),
                        after=[tk1])
            t_k1i = dma("gpsimd", colv[:, 40:44],
                        rowb.ap()[0:1, 0:512].rearrange(
                            "a (m p) -> (a p) m", p=128).opt(),
                        after=[t_k1o], slow=True)
            t_k3o = dma("gpsimd", rowb[0:1, 512:1024], srow(3, 512, 512),
                        after=[tk3, t_k1i])
            t_k3i = dma("gpsimd", colv[:, 44:48],
                        rowb.ap()[0:1, 512:1024].rearrange(
                            "a (m p) -> (a p) m", p=128).opt(),
                        after=[t_k3o], slow=True)

            t_qn = []
            tq_prev = t_cb
            for kt in range(NKT):
                ta = op("vector",
                        lambda eng, kt=kt, c0=c0: eng.tensor_tensor(
                            scr[:, :], xtile(kt, c0, 512),
                            slot(1)[:, 0:512], ALU.mult),
                        after=[t_rsb, t_cb, tq_prev, t_wqdma2,
                               t_wqdma2b] + scr_prev)
                ta2 = op("vector",
                         lambda eng: eng.tensor_tensor(
                             scr[:, :], scr[:, :], slot(0)[:, 0:512],
                             ALU.add), after=[ta])
                tb = op("vector",
                        lambda eng: eng.tensor_copy(sci[:, :], scr[:, :]),
                        after=[ta2])
                td = op("vector",
                        lambda eng: eng.tensor_tensor(
                            scr[:, :], scr[:, :], sci[:, :], ALU.subtract),
                        after=[tb])
                te = op("vector",
                        lambda eng: eng.tensor_single_scalar(
                            scr[:, :], scr[:, :], 0.5, ALU.is_gt),
                        after=[td])
                tc_ = op("vector",
                         lambda eng, kt=kt: eng.scalar_tensor_tensor(
                             qn[:, kt * 512:(kt + 1) * 512], sci[:, :],
                             -128.0, scr[:, :], ALU.add, ALU.add),
                         after=[te])
                t_qn.append(tc_)
                tq_prev = tc_
                scr_prev = [tc_]

            t_hmax0 = []
            for mt in range(4):
                t_hmax0.append(op(
                    "vector",
                    lambda eng, mt=mt: eng.memset(
                        colv[:, 56 + mt:57 + mt], 0.0), after=dep0))
            t_a_mm, t_b_mm, t_h = {}, {}, {}
            th_prev = None
            for ch in range(NCHUNK):
                depA = [t_agw1] + ([t_a_mm[(ch - 1, 3)]] if ch else dep0)
                depB = [t_agw3] + ([t_b_mm[(ch - 1, 3)]] if ch else dep0)
                tldA = dma("sync",
                           sbuf_stream(0).rearrange("p (k c) -> p k c", k=16),
                           w1g.ap()[:, ch * CHUNK:(ch + 1) * CHUNK].rearrange(
                               "(k p) c -> p k c", p=128), after=depA)
                tldB = dma("scalar",
                           sbuf_stream(1).rearrange("p (k c) -> p k c", k=16),
                           w3g.ap()[:, ch * CHUNK:(ch + 1) * CHUNK].rearrange(
                               "(k p) c -> p k c", p=128), after=depB)
                for mt in range(4):
                    depm = t_qn + [tldA]
                    if ch >= 1:
                        depm.append(t_h[(ch - 1, mt)])
                    else:
                        depm = depm + t_lncfree + t_hrev
                    t_a_mm[(ch, mt)] = op(
                        "tensor",
                        lambda eng, mt=mt: [
                            eng.matmul(PS[mt][:, 0:CHUNK],
                                       qn[:, kt * 512 + mt * 128:
                                          kt * 512 + (mt + 1) * 128],
                                       sbuf_stream(0)[:, kt * CHUNK:
                                                      (kt + 1) * CHUNK],
                                       start=(kt == 0),
                                       stop=(kt == NKT - 1))
                            for kt in range(NKT)][-1],
                        after=depm)
                    depb = [tldB] + ([t_h[(ch - 1, mt)]] if ch else
                                     t_qn + t_lncfree)
                    t_b_mm[(ch, mt)] = op(
                        "tensor",
                        lambda eng, mt=mt: [
                            eng.matmul(PS[4 + mt][:, 0:CHUNK],
                                       qn[:, kt * 512 + mt * 128:
                                          kt * 512 + (mt + 1) * 128],
                                       sbuf_stream(1)[:, kt * CHUNK:
                                                      (kt + 1) * CHUNK],
                                       start=(kt == 0),
                                       stop=(kt == NKT - 1))
                            for kt in range(NKT)][-1],
                        after=depb)
                    tsa = op("scalar",
                             lambda eng, mt=mt: eng.activation(
                                 scr[:, 0:CHUNK], PS[mt][:, 0:CHUNK],
                                 AF.Silu, scale=colv[:, 40 + mt:41 + mt]),
                             after=[t_a_mm[(ch, mt)], t_k1i] +
                             ([th_prev] if th_prev else scr_prev))
                    tsb = op("scalar",
                             lambda eng, mt=mt: eng.activation(
                                 scr[:, CHUNK:2 * CHUNK],
                                 PS[4 + mt][:, 0:CHUNK], AF.Copy,
                                 scale=colv[:, 44 + mt:45 + mt]),
                             after=[t_b_mm[(ch, mt)], t_k3i])
                    th = op("vector",
                            lambda eng, mt=mt, ch=ch: eng.tensor_tensor(
                                raf(mt * HID + ch * CHUNK, CHUNK),
                                scr[:, 0:CHUNK], scr[:, CHUNK:2 * CHUNK],
                                ALU.mult), after=[tsa, tsb])
                    tr = op("vector",
                            lambda eng, mt=mt, ch=ch: eng.reduce_max(
                                smv[0:128, 24 + mt:25 + mt],
                                raf(mt * HID + ch * CHUNK, CHUNK), AX.X,
                                apply_absolute_value=True),
                            after=[th] + ([t_h[(ch - 1, mt)]] if ch else []))
                    t_h[(ch, mt)] = op(
                        "vector",
                        lambda eng, mt=mt: eng.tensor_tensor(
                            colv[:, 56 + mt:57 + mt],
                            colv[:, 56 + mt:57 + mt],
                            smv[0:128, 24 + mt:25 + mt], ALU.max),
                        after=[tr, t_hmax0[mt]])
                    th_prev = th
            scr_prev = [th_prev]
            t_h_all = [t_h[(NCHUNK - 1, mt)] for mt in range(4)]

            ts1 = op("vector",
                     lambda eng: eng.tensor_single_scalar(
                         colv[:, 48:52], colv[:, 56:60], Q_EPS, ALU.max),
                     after=t_h_all)
            ts2 = op("vector",
                     lambda eng: eng.reciprocal(
                         colv[:, 48:52], colv[:, 48:52]), after=[ts1])
            ts3 = op("vector",
                     lambda eng: eng.tensor_single_scalar(
                         colv[:, 48:52], colv[:, 48:52], 127.0, ALU.mult),
                     after=[ts2])
            tk2 = op("vector",
                     lambda eng: eng.reciprocal(
                         colv[:, 52:56], colv[:, 48:52]), after=[ts3])
            tk2b = op("vector",
                      lambda eng: eng.tensor_scalar(
                          colv[:, 52:56], colv[:, 52:56], smv[:, 18:19],
                          None, ALU.mult), after=[tk2])
            t_k2o = dma("gpsimd",
                        rowb.ap()[0:1, 0:512].rearrange(
                            "a (m p) -> (a p) m", p=128).opt(),
                        colv[:, 52:56], after=[tk2b, t_k3i], slow=True)
            t_k2i = dma("gpsimd", srow(2, 0, 512), rowb[0:1, 0:512],
                        after=[t_k2o, t5])
            t_k2b = bcast_row(srow(2, 0, 512), slot(2)[:, 0:512], 512,
                              5, [t_k2i] + ps5_prev + t_h_all)
            ps5_prev = [t_k2b]

            qh = ra[:, 0:11264].bitcast(BF16)
            t_qh = {}
            tqp = [ts3] + t_h_all + scr_prev
            for mt in range(4):
                for cq in range(11):
                    ta = op("vector",
                            lambda eng, mt=mt, cq=cq: eng.tensor_scalar(
                                scr[:, :], raf(mt * HID + cq * 512, 512),
                                colv[:, 48 + mt:49 + mt], 128.0,
                                ALU.mult, ALU.add), after=tqp)
                    tb = op("vector",
                            lambda eng: eng.tensor_copy(sci[:, :],
                                                        scr[:, :]),
                            after=[ta])
                    td = op("vector",
                            lambda eng: eng.tensor_tensor(
                                scr[:, :], scr[:, :], sci[:, :],
                                ALU.subtract), after=[tb])
                    te = op("vector",
                            lambda eng: eng.tensor_single_scalar(
                                scr[:, :], scr[:, :], 0.5, ALU.is_gt),
                            after=[td])
                    tf = op("vector",
                            lambda eng, mt=mt, cq=cq:
                            eng.scalar_tensor_tensor(
                                qh[:, mt * HID + cq * 512:
                                   mt * HID + (cq + 1) * 512],
                                sci[:, :], -128.0, scr[:, :],
                                ALU.add, ALU.add), after=[te])
                    t_qh[(mt, cq)] = tf
                    tqp = [tf]

            qhT = ra[:, 11264:22528].bitcast(BF16)

            def qhT_t(kt):
                return qhT[:, kt * 512:(kt + 1) * 512]

            tp_bank_free = {0: None, 1: None}
            t_tp_all = []
            t_qh_last = t_qh[(3, 10)]
            for kt in range(HKT):
                for mt in range(4):
                    bank = (kt * 4 + mt) % 2
                    dep = [t_qh_last, t_idb] + t_h_all
                    if tp_bank_free[bank] is not None:
                        dep.append(tp_bank_free[bank])
                    tmm = op(
                        "tensor",
                        lambda eng, kt=kt, mt=mt, bank=bank: eng.transpose(
                            PS[2 + bank][:, 0:512].bitcast(BF16)[:, 0:128],
                            qh[:, mt * HID + kt * 128:
                               mt * HID + (kt + 1) * 128],
                            IDB[:]),
                        after=dep)
                    tev = op("scalar",
                             lambda eng, kt=kt, mt=mt, bank=bank: eng.copy(
                                 qhT_t(kt)[:, mt * 128:(mt + 1) * 128],
                                 PS[2 + bank][:, 0:512].bitcast(
                                     BF16)[:, 0:128]),
                             after=[tmm])
                    tp_bank_free[bank] = tev
                    t_tp_all.append(tev)
            t_tp_last = [t_tp_all[-1]]

            t_g2ev = []
            g2_buf_free = {0: None, 1: None}
            for grp in range(2):
                last_mm = None
                for kt in range(HKT):
                    sb_i = kt % 2
                    depL = [t_agw2] + t_tp_last
                    if g2_buf_free[sb_i] is not None:
                        depL.append(g2_buf_free[sb_i])
                    tld = dma("sync", sbuf_stream(sb_i)[:, 0:1024],
                              w2g.ap()[kt * 128:(kt + 1) * 128,
                                       grp * 1024:(grp + 1) * 1024],
                              after=depL, semname="dL%d" % sb_i)
                    depM = [tld]
                    if kt == 0:
                        depM = depM + (t_g2ev[-8:] if grp == 1 else
                                       t_tp_last + t_lncfree + [t_k2b])
                    last_mm = op(
                        "tensor",
                        lambda eng, kt=kt, sb_i=sb_i: [
                            eng.matmul(PS[m8][:, :],
                                       sbuf_stream(sb_i)[:, m8 * 128:
                                                         (m8 + 1) * 128],
                                       qhT_t(kt),
                                       start=(kt == 0),
                                       stop=(kt == HKT - 1))
                            for m8 in range(8)][-1],
                        after=depM)
                    g2_buf_free[sb_i] = last_mm
                for m8 in range(8):
                    mtd = grp * 8 + m8
                    te1 = op("vector",
                             lambda eng, m8=m8: eng.tensor_tensor(
                                 scr[:, :], PS[m8][:, :],
                                 slot(2)[:, 0:512], ALU.mult),
                             after=[last_mm, t_k2b] +
                             (t_g2ev[-1:] if t_g2ev else scr_prev))
                    te2 = op("vector",
                             lambda eng, mtd=mtd, c0=c0: eng.tensor_tensor(
                                 xtile(mtd, c0, 512), scr[:, :],
                                 xtile(mtd, c0, 512), ALU.add),
                             after=[te1])
                    t_g2ev.append(te2)
            scr_prev = t_g2ev[-1:]
            t_half_done[hf] = t_g2ev[-1]

        for kt in range(NKT):
            dma("sync", outT[kt * 128:(kt + 1) * 128, :], xtile(kt),
                after=[t_half_done[0], t_half_done[1]])
        fin = cnt["dA"]
        steps["sync"].append(lambda eng, fin=fin: eng.wait_ge(sem["dA"], fin))

        with nc.Block() as block:
            @block.sync
            def _(eng):
                for fn in steps["sync"]:
                    fn(eng)

            @block.vector
            def _(eng):
                for fn in steps["vector"]:
                    fn(eng)

            @block.scalar
            def _(eng):
                for fn in steps["scalar"]:
                    fn(eng)

            @block.gpsimd
            def _(eng):
                for fn in steps["gpsimd"]:
                    fn(eng)

            @block.tensor
            def _(eng):
                for fn in steps["tensor"]:
                    fn(eng)

    return nc


def _chain_masks(core):
    m = np.zeros((1, 64), np.float32)
    b, s = core // 4, core % 4
    for st in range(3):
        j = b * 4 + st
        if st < s:
            m[0, st * 8 + j] = 1.0
        else:
            m[0, 24 + st] = 1.0
    for st in range(3):
        j = b * 4 + (3 - st)
        if (3 - st) > s:
            m[0, 27 + st * 8 + j] = 1.0
        else:
            m[0, 51 + st] = 1.0
    return np.repeat(m, 128, axis=0).astype(np.float32)


def kernel(L0_attn_x, L0_attn_W_in, L0_attn_W_fwd, L0_attn_W_bwd,
           L0_attn_W_out, L0_attn_gamma, L0_attn_beta, L0_ffn_w1,
           L0_ffn_w2, L0_ffn_w3):
    x = np.ascontiguousarray(np.asarray(L0_attn_x, dtype=np.float32))
    W_in = np.ascontiguousarray(np.asarray(L0_attn_W_in, np.float32))
    W_f = np.ascontiguousarray(np.asarray(L0_attn_W_fwd, np.float32))
    W_b = np.ascontiguousarray(np.asarray(L0_attn_W_bwd, np.float32))
    W_o = np.ascontiguousarray(np.asarray(L0_attn_W_out, np.float32))
    gam = np.asarray(L0_attn_gamma, np.float32).reshape(NKT, 128).T.copy()
    bet = np.asarray(L0_attn_beta, np.float32).reshape(NKT, 128).T.copy()
    w1T = np.ascontiguousarray(np.asarray(L0_ffn_w1, np.float32).T)
    w3T = np.ascontiguousarray(np.asarray(L0_ffn_w3, np.float32).T)
    w2T = np.ascontiguousarray(np.asarray(L0_ffn_w2, np.float32).T)
    ones = np.ones((128, 1), np.float32)
    identb = np.eye(128, dtype=ml_dtypes.bfloat16)
    identf = np.eye(128, dtype=np.float32)
    onesrow = np.ones((1, 128), np.float32)

    nc = build_nc()
    in_maps = []
    for core in range(8):
        b, s = core // 4, core % 4
        xs = np.ascontiguousarray(x[b, s * T:(s + 1) * T, :].T)
        in_maps.append({
            "xT": xs,
            "W_in": W_in, "W_f": W_f, "W_b": W_b, "W_o": W_o,
            "gamP": gam, "betP": bet,
            "w1s": np.ascontiguousarray(w1T[core * 256:(core + 1) * 256, :]),
            "w3s": np.ascontiguousarray(w3T[core * 256:(core + 1) * 256, :]),
            "w2s": np.ascontiguousarray(w2T[core * 704:(core + 1) * 704, :]),
            "cmask": _chain_masks(core),
            "ones": ones,
            "identb": identb,
            "identf": identf,
            "onesrow": onesrow,
        })
    res = run_bass_kernel_spmd(nc, in_maps, core_ids=list(range(8)))
    out = np.empty((B, L, D), np.float32)
    for core in range(8):
        b, s = core // 4, core % 4
        out[b, s * T:(s + 1) * T, :] = res.results[core]["outT"].T
    return out


# version 26
# speedup vs baseline: 1.0590x; 1.0590x over previous
import numpy as np
import ml_dtypes
from contextlib import ExitStack

import concourse.bass as bass
import concourse.mybir as mybir
from concourse.bass_utils import run_bass_kernel_spmd

F32 = mybir.dt.float32
BF16 = mybir.dt.bfloat16
I32 = mybir.dt.int32
AF = mybir.ActivationFunctionType
ALU = mybir.AluOpType
AX = mybir.AxisListType

B, L, D, DI = 2, 4096, 2048, 256
HID = 5632
T = 1024
NKT = D // 128
LN_EPS = 1e-5
Q_EPS = 1e-5
NW1 = float(HID * D)
NW2 = float(D * HID)
CHUNK = 256
NCHUNK = HID // CHUNK
HKT = HID // 128
RG = [list(range(8))]


def build_nc():
    nc = bass.Bass("TRN2", target_bir_lowering=False, debug=False)

    xT = nc.dram_tensor("xT", [D, T], F32, kind="ExternalInput").ap()
    W_in = nc.dram_tensor("W_in", [D, DI], F32, kind="ExternalInput").ap()
    W_f = nc.dram_tensor("W_f", [DI, 2 * DI], F32, kind="ExternalInput").ap()
    W_b = nc.dram_tensor("W_b", [DI, 2 * DI], F32, kind="ExternalInput").ap()
    W_o = nc.dram_tensor("W_o", [2 * DI, D], F32, kind="ExternalInput").ap()
    gamP = nc.dram_tensor("gamP", [128, NKT], F32, kind="ExternalInput").ap()
    betP = nc.dram_tensor("betP", [128, NKT], F32, kind="ExternalInput").ap()
    w1s = nc.dram_tensor("w1s", [D // 8, HID], F32, kind="ExternalInput").ap()
    w3s = nc.dram_tensor("w3s", [D // 8, HID], F32, kind="ExternalInput").ap()
    w2s = nc.dram_tensor("w2s", [HID // 8, D], F32, kind="ExternalInput").ap()
    cmask = nc.dram_tensor("cmask", [128, 64], F32, kind="ExternalInput").ap()
    ones = nc.dram_tensor("ones", [128, 1], F32, kind="ExternalInput").ap()
    identb = nc.dram_tensor("identb", [128, 128], BF16,
                            kind="ExternalInput").ap()
    identf = nc.dram_tensor("identf", [128, 128], F32,
                            kind="ExternalInput").ap()
    onesrow = nc.dram_tensor("onesrow", [1, 128], F32,
                             kind="ExternalInput").ap()

    outT = nc.dram_tensor("outT", [D, T], F32, kind="ExternalOutput").ap()

    ag_sums_in = nc.dram_tensor("ag_sums_in", [1, 4], F32)
    ag_sums_out = nc.dram_tensor("ag_sums_out", [8, 4], F32)
    ag_scan_in = nc.dram_tensor("ag_scan_in", [128, 8], F32)
    ag_scan_out = nc.dram_tensor("ag_scan_out", [1024, 8], F32)
    ag_w1_in = nc.dram_tensor("ag_w1_in", [D // 8, HID], BF16)
    w1g = nc.dram_tensor("w1g", [D, HID], BF16)
    ag_w3_in = nc.dram_tensor("ag_w3_in", [D // 8, HID], BF16)
    w3g = nc.dram_tensor("w3g", [D, HID], BF16)
    ag_w2_in = nc.dram_tensor("ag_w2_in", [HID // 8, D], BF16)
    w2g = nc.dram_tensor("w2g", [HID, D], BF16)
    rowb = nc.dram_tensor("rowb", [1, T], F32)

    es = ExitStack()
    with es:
        XB = es.enter_context(nc.sbuf_tensor("XB", [128, NKT * T], F32))
        RA = es.enter_context(nc.sbuf_tensor("RA", [128, 22528], F32))
        QN = es.enter_context(nc.sbuf_tensor("QN", [128, 8192], BF16))
        SH = es.enter_context(nc.sbuf_tensor("SH", [128, 8192], F32))
        SCR = es.enter_context(nc.sbuf_tensor("SCR", [128, 512], F32))
        SCI = es.enter_context(nc.sbuf_tensor("SCI", [128, 512], I32))
        COLS = es.enter_context(nc.sbuf_tensor("COLS", [128, 64], F32))
        GATH = es.enter_context(nc.sbuf_tensor("GATH", [128, 80], F32))
        MASK = es.enter_context(nc.sbuf_tensor("MASK", [128, 64], F32))
        SMALL = es.enter_context(nc.sbuf_tensor("SMALL", [128, 32], F32))
        ONES = es.enter_context(nc.sbuf_tensor("ONES", [128, 1], F32))
        IDB = es.enter_context(nc.sbuf_tensor("IDB", [128, 128], BF16))
        IDF = es.enter_context(nc.sbuf_tensor("IDF", [128, 128], F32))
        ONESR = es.enter_context(nc.sbuf_tensor("ONESR", [1, 128], F32))
        PS = [es.enter_context(nc.psum_tensor(f"PS{i}", [128, 512], F32))
              for i in range(8)]
        sem = {n: es.enter_context(nc.semaphore(n))
               for n in ["dA", "dG", "dS", "TS", "VS", "SS", "GS", "CC",
                         "dW0", "dW1", "dT0", "dT1", "dL0", "dL1", "dX",
                         "dV0", "dV1", "dU0", "dU1", "dE"]}

        steps = {"sync": [], "vector": [], "scalar": [], "gpsimd": [],
                 "tensor": []}
        cnt = {}

        def bump(nm, amt=1):
            cnt[nm] = cnt.get(nm, 0) + amt
            return (nm, cnt[nm])

        QSEM = {"sync": "dA", "gpsimd": "dG", "scalar": "dS",
                "vector": "dE"}
        ESEM = {"vector": "VS", "scalar": "SS", "tensor": "TS",
                "gpsimd": "GS"}

        def dma(queue, out, in_, after=(), slow=False, semname=None):
            tok = bump(semname or QSEM[queue], 16)

            def fn(eng, out=out, in_=in_, after=tuple(after), tok=tok):
                for nm, v in after:
                    eng.wait_ge(sem[nm], v)
                eng.dma_start(out=out, in_=in_,
                              allow_slow_non_contiguous=slow
                              ).then_inc(sem[tok[0]], 16)
            steps[queue].append(fn)
            return tok

        def op(engine, fn, after=()):
            tok = bump(ESEM[engine])

            def wrapped(eng, fn=fn, after=tuple(after), tok=tok):
                for nm, v in after:
                    eng.wait_ge(sem[nm], v)
                fn(eng).then_inc(sem[tok[0]], 1)
            steps[engine].append(wrapped)
            return tok

        def cc(ins_h, outs_h, after=()):
            tok = bump("CC")

            def fn(eng, tok=tok, after=tuple(after)):
                for nm, v in after:
                    eng.wait_ge(sem[nm], v)
                eng.collective_compute(
                    "AllGather", ALU.bypass, replica_groups=RG,
                    ins=[ins_h.ap().opt()], outs=[outs_h.ap().opt()]
                ).then_inc(sem["CC"], 1)
            steps["gpsimd"].append(fn)
            return tok

        ra, shp, xb, qn = RA.ap(), SH.ap(), XB.ap(), QN.ap()
        colv, smv, gat, msk = COLS.ap(), SMALL.ap(), GATH.ap(), MASK.ap()
        scr, sci = SCR.ap(), SCI.ap()

        def raf(a, n):
            return ra[:, a:a + n]

        def rabf(a, n):
            return ra[:, a:a + (n + 1) // 2].bitcast(BF16)[:, 0:n]

        def rai32(a, n):
            return ra[:, a:a + n].bitcast(I32)[:, 0:n]

        def slot(i):
            return shp[:, i * T:(i + 1) * T]

        def srow(i, a=0, n=T):
            return shp[0:1, i * T + a:i * T + a + n]

        def sbuf_stream(i):
            return shp[:, 4096 + i * 2048:4096 + (i + 1) * 2048].bitcast(
                BF16)[:, 0:4096]

        def xtile(kt, c0=0, cn=T):
            return xb[:, kt * T + c0:kt * T + c0 + cn]

        O_WIN, O_WFB, O_Z, O_ZR, O_DG, O_P, O_H = (
            0, 4096, 6144, 8192, 10240, 14336, 18432)
        O_Y = 8192
        O_HR = 16384
        qnf = qn[:, 0:8192].bitcast(F32)

        def wqa(i):
            return qnf[:, i * 1024:(i + 1) * 1024]

        wqf2 = qnf[:, 2048:3072]
        wqf3 = qnf[:, 3072:4096]

        def wqb(i):
            return shp[:, 6144 + i * 512:6144 + (i + 1) * 512].bitcast(
                BF16)[:, 0:1024]

        dma("sync", ONES[:], ones)
        dma("sync", colv[:, 0:NKT], gamP)
        dma("sync", colv[:, NKT:2 * NKT], betP)
        dma("sync", msk[:, :], cmask)
        dma("sync", IDB[:], identb)
        dma("sync", IDF[:], identf)
        dma("sync", ONESR[:], onesrow)
        for kt in range(8):
            dma("sync", xtile(kt), xT[kt * 128:(kt + 1) * 128, :])
        tBAT1 = ("dA", cnt["dA"])
        for kt in range(8, NKT):
            dma("scalar", xtile(kt), xT[kt * 128:(kt + 1) * 128, :],
                semname="dE")
        tBATX = ("dE", cnt["dE"])
        dma("sync", raf(O_WIN, 4096).rearrange("p (k c) -> p k c", k=16),
            W_in.rearrange("(k p) c -> p k c", p=128), after=[tBAT1])
        dma("sync", raf(O_WFB, 1024).rearrange("p (k c) -> p k c", k=2),
            W_f.rearrange("(k p) c -> p k c", p=128))
        dma("sync", raf(O_WFB + 1024, 1024).rearrange("p (k c) -> p k c", k=2),
            W_b.rearrange("(k p) c -> p k c", p=128))
        tINIT2 = ("dA", cnt["dA"])
        t_x = [tBAT1, tBATX]
        t_ones = t_gam = t_bet = t_mask = t_idb = t_idf = t_onr = tBAT1
        t_win = t_wf = t_wb = tINIT2

        def wchunks(ncols):
            out, a = [], 0
            while a < ncols:
                n = min(1024, ncols - a)
                out.append((a, n))
                a += n
            return out

        wq_buf_free = {}

        def wsum_pass(wap, nrows, ncols, acc_col, queue, semn, bufn, bufs,
                      prev=()):
            acc = smv[:, acc_col:acc_col + 1]
            tred = op("vector", lambda eng: eng.memset(acc, 0.0),
                      after=list(prev))
            ci = 0
            for pt in range((nrows + 127) // 128):
                pr = min(128, nrows - pt * 128)
                for (a, n) in wchunks(ncols):
                    bi = ci % 2
                    key = bufn + str(bi)
                    dep = ([wq_buf_free[key]] if wq_buf_free.get(key)
                           else [])
                    tl = dma(queue, bufs[bi][0:pr, 0:n],
                             wap[pt * 128:pt * 128 + pr, a:a + n], after=dep,
                             semname="%s%d" % (semn, bi))
                    tr = op("vector",
                            lambda eng, pr=pr, n=n, bi=bi, bufs=bufs:
                            eng.reduce_sum(
                                scr[0:pr, 0:1], bufs[bi][0:pr, 0:n], AX.X,
                                apply_absolute_value=True),
                            after=[tl, tred])
                    tred = op("vector",
                              lambda eng, pr=pr: eng.tensor_tensor(
                                  acc[0:pr, :], acc[0:pr, :],
                                  scr[0:pr, 0:1], ALU.add),
                              after=[tr, tred])
                    wq_buf_free[key] = tr
                    ci += 1
            return tred

        shbuf = [shp[:, 4096:5120], shp[:, 5120:6144]]
        t_r1_ = wsum_pass(w1s, 256, HID, 0, "gpsimd", "dW", "W",
                          [wqa(0), wqa(1)])
        t_r3_ = wsum_pass(w3s, 256, HID, 1, "sync", "dV", "V",
                          [wqa(2), wqa(3)], prev=[t_r1_])
        t_r2_ = wsum_pass(w2s, 704, D, 2, "scalar", "dU", "U", shbuf,
                          prev=[t_r3_])

        def ln_stats(src, ncols, sq_scratch, mu_row, r_row, scratch_row,
                     after=()):
            nnc = ncols // 512
            after = list(after)
            tmm = op("tensor",
                     lambda eng: [
                         eng.matmul(PS[4 + nc_][0:1, :], ONES[:],
                                    src(kt)[:, nc_ * 512:(nc_ + 1) * 512],
                                    start=(kt == 0), stop=(kt == NKT - 1))
                         for nc_ in range(nnc) for kt in range(NKT)][-1],
                     after=after + [t_ones])
            tq_mm = None
            for kt in range(NKT):
                tS = op("scalar",
                        lambda eng, kt=kt: eng.activation(
                            sq_scratch, src(kt), AF.Square),
                        after=after + ([tq_mm] if tq_mm else []))
                tq_mm = op("tensor",
                           lambda eng, kt=kt: [
                               eng.matmul(
                                   PS[6 + nc_][0:1, :], ONES[:],
                                   sq_scratch[:, nc_ * 512:(nc_ + 1) * 512],
                                   start=(kt == 0), stop=(kt == NKT - 1))
                               for nc_ in range(nnc)][-1],
                           after=[tS])
            tmu = []
            tms = []
            for nc_ in range(nnc):
                tmu.append(op(
                    "vector",
                    lambda eng, nc_=nc_: eng.tensor_scalar(
                        mu_row[0:1, nc_ * 512:(nc_ + 1) * 512],
                        PS[4 + nc_][0:1, :], 1.0 / D, None, ALU.mult),
                    after=[tmm]))
                tms.append(op(
                    "vector",
                    lambda eng, nc_=nc_: eng.tensor_scalar(
                        scratch_row[0:1, nc_ * 512:(nc_ + 1) * 512],
                        PS[6 + nc_][0:1, :], 1.0 / D, None, ALU.mult),
                    after=[tq_mm]))
            tv1 = op("vector",
                     lambda eng: eng.tensor_tensor(
                         r_row, mu_row, mu_row, ALU.mult), after=tmu)
            tv2 = op("vector",
                     lambda eng: eng.tensor_tensor(
                         scratch_row, scratch_row, r_row, ALU.subtract),
                     after=tms + [tv1])
            tv3 = op("vector",
                     lambda eng: eng.tensor_single_scalar(
                         scratch_row, scratch_row, LN_EPS, ALU.add),
                     after=[tv2])
            tl1 = op("scalar",
                     lambda eng: eng.activation(r_row, scratch_row, AF.Ln),
                     after=[tv3])
            tl2 = op("scalar",
                     lambda eng: eng.activation(r_row, r_row, AF.Exp,
                                                scale=-0.5), after=[tl1])
            tfree = tmu + tms
            return tl2, tfree

        t_r1, t_ln1free = ln_stats(lambda kt: xtile(kt), 1024, slot(2),
                                   srow(0), srow(1), srow(2), after=t_x)
        t_mur1 = op("vector",
                    lambda eng: eng.tensor_tensor(
                        srow(2), srow(0), srow(1), ALU.mult), after=[t_r1])
        def bcast_row(row_ap, dst_ap, n, bank, after):
            tmm = op("tensor",
                     lambda eng: eng.matmul(PS[bank][:, 0:n], ONESR[:],
                                            row_ap, start=True, stop=True),
                     after=list(after) + [t_onr])
            return op("scalar",
                      lambda eng: eng.copy(dst_ap, PS[bank][:, 0:n]),
                      after=[tmm])

        tb0 = bcast_row(srow(1, 0, 512), slot(0)[:, 0:512], 512, 2,
                        [t_r1, t_mur1])
        t_r1b = bcast_row(srow(1, 512, 512), slot(0)[:, 512:1024], 512, 3,
                          [t_r1, tb0])
        tb1 = bcast_row(srow(2, 0, 512), slot(1)[:, 0:512], 512, 2,
                        [t_mur1, t_r1b])
        t_mur1b = bcast_row(srow(2, 512, 512), slot(1)[:, 512:1024], 512, 3,
                            [t_mur1, tb1])

        def u1t(i):
            return raf(O_DG + i * 1024, 1024)

        t_u1_mm = {}
        for kt in range(NKT):
            buf = kt % 2
            dep = [t_r1b, t_mur1b]
            if kt >= 2:
                dep.append(t_u1_mm[kt - 2])
            ta = op("vector",
                    lambda eng, kt=kt, buf=buf: eng.tensor_tensor(
                        u1t(buf), xtile(kt), slot(0), ALU.mult), after=dep)
            tb = op("vector",
                    lambda eng, buf=buf: eng.tensor_tensor(
                        u1t(buf), u1t(buf), slot(1), ALU.subtract),
                    after=[ta])
            t_u1_mm[kt] = op(
                "tensor",
                lambda eng, kt=kt, buf=buf: [
                    eng.matmul(PS[mt * 2 + nc_][:, :],
                               raf(O_WIN + kt * 256 + mt * 128, 128),
                               u1t(buf)[:, nc_ * 512:(nc_ + 1) * 512],
                               start=(kt == 0), stop=(kt == NKT - 1))
                    for mt in range(2) for nc_ in range(2)][-1],
                after=[tb, t_win] + ([] if kt else [t_r1b, t_mur1b]))
        t_zev = []
        for mt in range(2):
            for nc_ in range(2):
                t_zev.append(op(
                    "scalar",
                    lambda eng, mt=mt, nc_=nc_: eng.copy(
                        raf(O_Z + mt * 1024 + nc_ * 512, 512),
                        PS[mt * 2 + nc_][:, :]),
                    after=[t_u1_mm[NKT - 1]]))

        t_zrev = []
        for g in range(2):
            t_zrev.append(op(
                "vector",
                lambda eng, g=g: eng.tensor_copy(
                    raf(O_ZR + g * 1024, 1024),
                    raf(O_Z + g * 1024, 1024)[:, ::-1]),
                after=t_zev))

        t_zero = op("vector",
                    lambda eng: eng.memset(raf(O_WIN + 1024, 1024), 0.0),
                    after=[t_u1_mm[NKT - 1]])
        zeros1024 = raf(O_WIN + 1024, 1024)

        t_smm = op("tensor",
                   lambda eng: eng.matmul(PS[7][0:1, 0:3], ONES[:],
                                          smv[:, 0:3], start=True, stop=True),
                   after=[t_r1_, t_r3_, t_r2_, t_ones] + t_ln1free)
        t_s2 = op("scalar",
                  lambda eng: eng.copy(smv[0:1, 0:3], PS[7][0:1, 0:3]),
                  after=[t_smm])
        t_z3 = op("vector", lambda eng: eng.memset(smv[0:1, 3:4], 0.0))
        t_dsum = dma("gpsimd", ag_sums_in[0:1, 0:4], smv[0:1, 0:4],
                     after=[t_s2, t_z3])
        t_ccs = cc(ag_sums_in, ag_sums_out, after=[t_dsum])
        t_dsb = dma("gpsimd", smv[0:8, 8:12], ag_sums_out[:, :],
                    after=[t_ccs])
        t_tot = op("tensor",
                   lambda eng: eng.matmul(PS[6][0:1, 0:4], ONES[0:8, :],
                                          smv[0:8, 8:12], start=True,
                                          stop=True),
                   after=[t_dsb, t_ones] + t_ln1free)
        t_totr = op("scalar",
                    lambda eng: eng.copy(smv[0:1, 8:12], PS[6][0:1, 0:4]),
                    after=[t_tot])
        t_bcm = op("tensor",
                   lambda eng: eng.matmul(PS[6][:, 0:4], ONESR[:],
                                          smv[0:1, 8:12], start=True,
                                          stop=True),
                   after=[t_totr, t_onr])
        t_t128 = op("scalar",
                    lambda eng: eng.copy(smv[:, 12:16], PS[6][:, 0:4]),
                    after=[t_bcm])
        t_mwa = op("vector",
                   lambda eng: eng.tensor_scalar(
                       smv[:, 16:17], smv[:, 12:13], 1.0 / NW1, Q_EPS,
                       ALU.mult, ALU.max), after=[t_t128])
        t_mwb = op("vector",
                   lambda eng: eng.tensor_scalar(
                       smv[:, 17:18], smv[:, 13:14], 1.0 / NW1, Q_EPS,
                       ALU.mult, ALU.max), after=[t_t128])
        t_mwc = op("vector",
                   lambda eng: eng.tensor_scalar(
                       smv[:, 18:19], smv[:, 14:15], 1.0 / NW2, Q_EPS,
                       ALU.mult, ALU.max), after=[t_t128])
        t_wsc = op("vector",
                   lambda eng: eng.tensor_single_scalar(
                       smv[:, 20:23], smv[:, 16:19], 0.5, ALU.mult),
                   after=[t_mwa, t_mwb, t_mwc])

        wqb_free = {}

        def wquant_pass(wap, nrows, ncols, scol, ag_in_h, ag_out_h,
                        defer_cc=False):
            tdma_last = None
            tq3_prev = None
            ci = 0
            for pt in range((nrows + 127) // 128):
                pr = min(128, nrows - pt * 128)
                for (a, n) in wchunks(ncols):
                    bi = ci % 2
                    key = "W%d" % bi
                    dep = [t_wsc] + ([wq_buf_free[key]]
                                     if wq_buf_free.get(key) else [])
                    tl = dma("sync" if bi == 0 else "scalar",
                             wqa(bi)[0:pr, 0:n],
                             wap[pt * 128:pt * 128 + pr, a:a + n], after=dep,
                             semname="dW%d" % bi)
                    tq1 = op("scalar",
                             lambda eng, pr=pr, n=n, bi=bi:
                             eng.activation(
                                 wqf2[0:pr, 0:n], wqa(bi)[0:pr, 0:n],
                                 AF.Abs),
                             after=[tl, t_wsc] +
                             ([tq3_prev] if tq3_prev else []))
                    tq2 = op("scalar",
                             lambda eng, pr=pr, n=n, bi=bi:
                             eng.activation(
                                 wqf3[0:pr, 0:n], wqa(bi)[0:pr, 0:n],
                                 AF.Sign), after=[tq1])
                    tq2v = op("vector",
                              lambda eng, pr=pr, n=n, scol=scol:
                              eng.tensor_scalar(
                                  wqf2[0:pr, 0:n], wqf2[0:pr, 0:n],
                                  smv[0:pr, 20 + scol:21 + scol], None,
                                  ALU.is_gt),
                              after=[tq1] +
                              ([tq3_prev] if tq3_prev else []))
                    tq3 = op("vector",
                             lambda eng, pr=pr, n=n, bi=bi: eng.tensor_tensor(
                                 wqb(bi)[0:pr, 0:n], wqf2[0:pr, 0:n],
                                 wqf3[0:pr, 0:n], ALU.mult),
                             after=[tq2v, tq2] +
                             ([wqb_free[bi]] if wqb_free.get(bi) else []))
                    tdma_last = dma("sync" if bi == 0 else "scalar",
                                    ag_in_h[pt * 128:pt * 128 + pr, a:a + n],
                                    wqb(bi)[0:pr, 0:n], after=[tq3],
                                    semname="dT%d" % bi)
                    wq_buf_free["W%d" % bi] = tq2
                    wqb_free[bi] = tdma_last
                    tq3_prev = tq3
                    ci += 1
            tboth = [("dT0", cnt.get("dT0", 0)), ("dT1", cnt.get("dT1", 0))]
            if defer_cc:
                return tboth, tboth[0]
            return cc(ag_in_h, ag_out_h, after=tboth), tboth[0]



        t_scan_done = []
        t_extr = []
        prev_dg_ev = []
        for dire in range(2):
            zin = raf(O_Z, 2048) if dire == 0 else raf(O_ZR, 2048)
            wofs = O_WFB + dire * 1024
            tw = t_wf if dire == 0 else t_wb
            t_dg = []
            for nc_ in range(2):
                dep_mm = [tw] + t_zev + t_zrev + prev_dg_ev + t_dg
                tmm = op(
                    "tensor",
                    lambda eng, nc_=nc_, zin=zin, wofs=wofs: [
                        eng.matmul(PS[mt][:, :],
                                   raf(wofs + kt * 512 + mt * 128, 128),
                                   zin[:, kt * 1024 + nc_ * 512:
                                       kt * 1024 + (nc_ + 1) * 512],
                                   start=(kt == 0), stop=(kt == 1))
                        for mt in range(4) for kt in range(2)][-1],
                    after=dep_mm)
                for mt in range(2):
                    t_dg.append(op(
                        "scalar",
                        lambda eng, mt=mt, nc_=nc_: eng.activation(
                            raf(O_DG + mt * 1024 + nc_ * 512, 512),
                            PS[mt][:, :], AF.Sigmoid),
                        after=[tmm] + t_scan_done))
                for mt in range(2, 4):
                    t_dg.append(op(
                        "vector",
                        lambda eng, mt=mt, nc_=nc_, zin=zin:
                        eng.tensor_tensor(
                            raf(O_DG + 2048 + (mt - 2) * 1024 + nc_ * 512,
                                512),
                            PS[mt][:, :],
                            zin[:, (mt - 2) * 1024 + nc_ * 512:
                                (mt - 2) * 1024 + (nc_ + 1) * 512],
                            ALU.mult),
                        after=[tmm] + t_scan_done))
            prev_dg_ev = list(t_dg)
            t_sc = []
            for g in range(2):
                extra = []
                t_sc.append(op(
                    "vector",
                    lambda eng, g=g, dire=dire: eng.tensor_tensor_scan(
                        raf(O_H + dire * 2048 + g * 1024, 1024),
                        raf(O_DG + g * 1024, 1024),
                        raf(O_DG + 2048 + g * 1024, 1024),
                        0.0, ALU.mult, ALU.add),
                    after=t_dg + extra))
                t_sc.append(op(
                    "vector",
                    lambda eng, g=g, dire=dire: eng.tensor_tensor_scan(
                        raf(O_P + dire * 2048 + g * 1024, 1024),
                        raf(O_DG + g * 1024, 1024),
                        zeros1024, 1.0, ALU.mult, ALU.add),
                    after=t_dg + [t_zero] + extra))
            for g in range(2):
                t_extr.append(op(
                    "vector",
                    lambda eng, g=g, dire=dire: eng.tensor_copy(
                        gat[:, dire * 4 + 2 * g:dire * 4 + 2 * g + 1],
                        raf(O_P + dire * 2048 + g * 1024 + 1023, 1)),
                    after=t_sc))
                t_extr.append(op(
                    "vector",
                    lambda eng, g=g, dire=dire: eng.tensor_copy(
                        gat[:, dire * 4 + 2 * g + 1:dire * 4 + 2 * g + 2],
                        raf(O_H + dire * 2048 + g * 1024 + 1023, 1)),
                    after=t_sc))
            t_scan_done = t_scan_done + t_sc

        w1deps, _ = wquant_pass(w1s, 256, HID, 0, ag_w1_in, w1g,
                                defer_cc=True)
        w3deps, _ = wquant_pass(w3s, 256, HID, 1, ag_w3_in, w3g,
                                defer_cc=True)
        t_dsc = dma("sync", ag_scan_in[:, :], gat[:, 0:8], after=t_extr,
                    semname="dX")
        t_ccscan = cc(ag_scan_in, ag_scan_out, after=[t_dsc])
        t_dscb = dma("sync", gat[:, 16:80].rearrange("p (c q) -> p c q", c=8),
                     ag_scan_out.ap().rearrange("(c p) q -> p c q", p=128),
                     after=[t_ccscan], semname="dX")
        t_agw1 = cc(ag_w1_in, w1g, after=w1deps)
        t_agw3 = cc(ag_w3_in, w3g, after=w3deps)

        t_J = {}
        for dire in range(2):
            J = smv[:, 26 + dire * 2:28 + dire * 2]
            A = gat[:, 8:10]
            Ssel = gat[:, 10:12]
            tj = op("vector", lambda eng, J=J: eng.memset(J, 0.0),
                    after=[t_dscb, t_wsc] + ([t_J[0]] if dire else []))
            mbase = dire * 27
            for s in range(3):
                ta = op("vector",
                        lambda eng, s=s, mbase=mbase, A=A: eng.tensor_copy(
                            A[:, 0:1],
                            msk[:, mbase + 24 + s:mbase + 25 + s]),
                        after=[tj, t_mask])
                ta = op("vector",
                        lambda eng, s=s, mbase=mbase, A=A: eng.tensor_copy(
                            A[:, 1:2],
                            msk[:, mbase + 24 + s:mbase + 25 + s]),
                        after=[ta])
                ts_ = op("vector",
                         lambda eng, Ssel=Ssel: eng.memset(Ssel, 0.0),
                         after=[tj])
                for j in range(8):
                    q0 = 16 + j * 8 + dire * 4
                    ta = op("vector",
                            lambda eng, q0=q0, s=s, j=j, mbase=mbase, A=A:
                            eng.scalar_tensor_tensor(
                                A, gat[:, q0:q0 + 3:2],
                                msk[:, mbase + s * 8 + j:
                                    mbase + s * 8 + j + 1],
                                A, ALU.mult, ALU.add), after=[ta])
                    ts_ = op("vector",
                             lambda eng, q0=q0, s=s, j=j, mbase=mbase,
                             Ssel=Ssel: eng.scalar_tensor_tensor(
                                 Ssel, gat[:, q0 + 1:q0 + 4:2],
                                 msk[:, mbase + s * 8 + j:
                                     mbase + s * 8 + j + 1],
                                 Ssel, ALU.mult, ALU.add), after=[ts_])
                tj = op("vector",
                        lambda eng, J=J, A=A: eng.tensor_tensor(
                            J, J, A, ALU.mult), after=[ta, tj])
                tj = op("vector",
                        lambda eng, J=J, Ssel=Ssel: eng.tensor_tensor(
                            J, J, Ssel, ALU.add), after=[ts_, tj])
            t_J[dire] = tj

        t_hfix = []
        for dire in range(2):
            for g in range(2):
                t_hfix.append(op(
                    "vector",
                    lambda eng, dire=dire, g=g: eng.scalar_tensor_tensor(
                        raf(O_H + dire * 2048 + g * 1024, 1024),
                        raf(O_P + dire * 2048 + g * 1024, 1024),
                        smv[:, 26 + dire * 2 + g:27 + dire * 2 + g],
                        raf(O_H + dire * 2048 + g * 1024, 1024),
                        ALU.mult, ALU.add),
                    after=[t_J[dire]] + t_scan_done))
        t_hrev = []
        for g in range(2):
            t_hrev.append(op(
                "vector",
                lambda eng, g=g: eng.tensor_copy(
                    raf(O_HR + g * 1024, 1024),
                    raf(O_H + 2048 + g * 1024, 1024)[:, ::-1]),
                after=t_hfix))

        def hgrp(kt):
            return raf(O_H + kt * 1024, 1024) if kt < 2 else \
                raf(O_HR + (kt - 2) * 1024, 1024)

        t_wo = dma("sync",
                   raf(O_WIN, 8192).rearrange("p (k c) -> p k c", k=4),
                   W_o.rearrange("(k p) c -> p k c", p=128),
                   after=t_scan_done + t_zrev)

        def make_yprep(hf, extra):
            c0 = hf * 512
            vp, yp = None, {}
            for mtd in range(NKT):
                depv = [t_r1b, t_mur1b] + (t_hfix if mtd >= 12
                                           else t_scan_done) + list(extra)
                if vp:
                    depv.append(vp)
                v1 = op("vector",
                        lambda eng, mtd=mtd, c0=c0: eng.tensor_tensor(
                            raf(O_Y + mtd * 512, 512), xtile(mtd, c0, 512),
                            slot(0)[:, c0:c0 + 512], ALU.mult), after=depv)
                v2 = op("vector",
                        lambda eng, mtd=mtd, c0=c0: eng.tensor_tensor(
                            raf(O_Y + mtd * 512, 512),
                            raf(O_Y + mtd * 512, 512),
                            slot(1)[:, c0:c0 + 512], ALU.subtract),
                        after=[v1])
                yp[mtd] = v2
                vp = v2
            return yp

        yprep0 = make_yprep(0, [])

        t_x1 = {}
        prev_bank_ev = {0: None, 1: None}
        ps5_prev = t_ln1free
        scr_prev = [t_wsc]
        for hf in range(2):
            c0 = hf * 512
            t_yprep = yprep0 if hf == 0 else make_yprep(1, [t_x1[0]])
            t_yev = []
            for mtd in range(NKT):
                bank = mtd % 2
                dep = [t_wo, t_idf, t_yprep[mtd]] + t_hrev + t_hfix
                if prev_bank_ev[bank] is not None:
                    dep.append(prev_bank_ev[bank])
                tmm = op(
                    "tensor",
                    lambda eng, mtd=mtd, bank=bank, c0=c0: ([
                        eng.matmul(PS[bank][:, :],
                                   raf(O_WIN + kt * 2048 + mtd * 128, 128),
                                   hgrp(kt)[:, c0:c0 + 512],
                                   start=(kt == 0), stop=False)
                        for kt in range(4)] and
                        eng.matmul(PS[bank][:, :], IDF[:],
                                   raf(O_Y + mtd * 512, 512),
                                   start=False, stop=True)),
                    after=dep)
                te3 = op("scalar",
                         lambda eng, mtd=mtd, bank=bank: eng.copy(
                             raf(O_Y + mtd * 512, 512), PS[bank][:, :]),
                         after=[tmm])
                prev_bank_ev[bank] = te3
                t_yev.append(te3)

            def ysrc(kt):
                return raf(O_Y + kt * 512, 512)

            t_rby, t_lnbfree = ln_stats(
                ysrc, 512, slot(2)[:, 0:512], scr[0:1, 0:512],
                srow(2, 0, 512), srow(3, 0, 512),
                after=t_yev + scr_prev + t_ln1free)
            scr_prev = [t_lnbfree[0]]
            t_mury = op("vector",
                        lambda eng: eng.tensor_tensor(
                            scr[0:1, 0:512], scr[0:1, 0:512],
                            srow(2, 0, 512), ALU.mult), after=[t_rby])
            t_ryb = bcast_row(srow(2, 0, 512), slot(3)[:, 0:512], 512,
                              5, [t_rby] + ps5_prev)
            t_muryb = bcast_row(scr[0:1, 0:512], slot(2)[:, 0:512], 512,
                                5, [t_mury, t_ryb])
            ps5_prev = [t_muryb]
            tprev = t_muryb
            for kt in range(NKT):
                ta = op("vector",
                        lambda eng, kt=kt: eng.tensor_tensor(
                            scr[:, :], raf(O_Y + kt * 512, 512),
                            slot(3)[:, 0:512], ALU.mult),
                        after=[t_ryb, t_muryb, t_gam, t_bet, tprev,
                               t_mury])
                tb = op("vector",
                        lambda eng: eng.tensor_tensor(
                            scr[:, :], scr[:, :], slot(2)[:, 0:512],
                            ALU.subtract), after=[ta])
                tc_ = op("vector",
                         lambda eng, kt=kt, c0=c0: eng.scalar_tensor_tensor(
                             xtile(kt, c0, 512), scr[:, :],
                             colv[:, kt:kt + 1], xtile(kt, c0, 512),
                             ALU.mult, ALU.add), after=[tb])
                td = op("vector",
                        lambda eng, kt=kt, c0=c0: eng.tensor_scalar(
                            xtile(kt, c0, 512), xtile(kt, c0, 512),
                            colv[:, NKT + kt:NKT + kt + 1], None, ALU.add),
                        after=[tc_])
                tprev = td
            scr_prev = [tprev]
            t_x1[hf] = tprev

        t_agw2, t_wqdma2 = wquant_pass(w2s, 704, D, 2, ag_w2_in, w2g)
        t_wqdma2b = ("dT1", cnt.get("dT1", 0))

        t_half_done = {}
        for hf in range(2):
            c0 = hf * 512
            dep0 = [t_x1[0], t_x1[1]]
            if hf == 1:
                dep0 = dep0 + [t_half_done[0]]

            def x1src(kt, c0=c0):
                return xtile(kt, c0, 512)

            t_r2, t_lncfree = ln_stats(
                x1src, 512, slot(2)[:, 0:512], scr[0:1, 0:512],
                srow(0, 0, 512), srow(1, 0, 512), after=dep0 + scr_prev)
            t_mu2b = bcast_row(scr[0:1, 0:512], slot(1)[:, 0:512], 512,
                               5, [t_r2] + ps5_prev)
            ps5_prev = [t_mu2b]
            tm = op("vector",
                    lambda eng: eng.memset(slot(3)[:, 0:512], 0.0),
                    after=[t_mu2b] + t_lncfree + dep0)
            for kt in range(NKT):
                ta = op("vector",
                        lambda eng, kt=kt, c0=c0: eng.tensor_tensor(
                            slot(2)[:, 0:512], xtile(kt, c0, 512),
                            slot(1)[:, 0:512], ALU.subtract),
                        after=[t_mu2b, tm] + t_lncfree)
                tm = op("vector",
                        lambda eng: eng.tensor_tensor(
                            slot(3)[:, 0:512], slot(3)[:, 0:512],
                            slot(2)[:, 0:512], ALU.max), after=[ta])
                tm = op("vector",
                        lambda eng: eng.scalar_tensor_tensor(
                            slot(3)[:, 0:512], slot(2)[:, 0:512], -1.0,
                            slot(3)[:, 0:512], ALU.mult, ALU.max),
                        after=[tm])
            tmx = tm
            for blk in range(4):
                tpm = op("tensor",
                         lambda eng, blk=blk: eng.transpose(
                             PS[5][:, 0:128],
                             slot(3)[:, blk * 128:(blk + 1) * 128],
                             IDF[:]),
                         after=[tmx, t_idf] + ps5_prev)
                tmx = op("vector",
                         lambda eng, blk=blk: eng.reduce_max(
                             colv[:, 60 + blk:61 + blk], PS[5][:, 0:128],
                             AX.X), after=[tpm])
                ps5_prev = [tmx]
            t_mxo = dma("gpsimd",
                        rowb.ap()[0:1, 0:512].rearrange(
                            "a (m p) -> (a p) m", p=128).opt(),
                        colv[:, 60:64], after=[tmx], slow=True)
            t_mxi = dma("gpsimd", srow(1, 512, 512), rowb[0:1, 0:512],
                        after=[t_mxo])
            t1 = op("vector",
                    lambda eng: eng.tensor_tensor(
                        srow(1, 512, 512), srow(1, 512, 512),
                        srow(0, 0, 512), ALU.mult), after=[t_mxi, t_r2])
            t2 = op("vector",
                    lambda eng: eng.tensor_single_scalar(
                        srow(1, 512, 512), srow(1, 512, 512), Q_EPS,
                        ALU.max), after=[t1])
            t3 = op("vector",
                    lambda eng: eng.reciprocal(
                        srow(2, 0, 512), srow(1, 512, 512)), after=[t2])
            t4 = op("vector",
                    lambda eng: eng.tensor_single_scalar(
                        srow(2, 0, 512), srow(2, 0, 512), 127.0, ALU.mult),
                    after=[t3])
            t5 = op("vector",
                    lambda eng: eng.tensor_single_scalar(
                        srow(2, 512, 512), srow(1, 512, 512), 1.0 / 127.0,
                        ALU.mult), after=[t4])
            tk1 = op("vector",
                     lambda eng: eng.tensor_scalar(
                         srow(3, 0, 512), srow(2, 512, 512),
                         smv[0:1, 16:17], None, ALU.mult), after=[t5])
            tk3 = op("vector",
                     lambda eng: eng.tensor_scalar(
                         srow(3, 512, 512), srow(2, 512, 512),
                         smv[0:1, 17:18], None, ALU.mult), after=[t5])
            trs = op("vector",
                     lambda eng: eng.tensor_tensor(
                         srow(0, 0, 512), srow(0, 0, 512), srow(2, 0, 512),
                         ALU.mult), after=[t4, tk1, tk3, t1])
            tc1 = op("vector",
                     lambda eng: eng.tensor_tensor(
                         scr[0:1, 0:512], scr[0:1, 0:512], srow(0, 0, 512),
                         ALU.mult), after=[trs, t_mu2b])
            tc2 = op("vector",
                     lambda eng: eng.tensor_scalar(
                         scr[0:1, 0:512], scr[0:1, 0:512], -1.0, 128.0,
                         ALU.mult, ALU.add), after=[tc1])
            t_rsb = bcast_row(srow(0, 0, 512), slot(1)[:, 0:512], 512,
                              5, [trs, t_mu2b] + ps5_prev)
            t_cb = bcast_row(scr[0:1, 0:512], slot(0)[:, 0:512], 512,
                             5, [tc2, t_rsb])
            ps5_prev = [t_cb]
            t_k1o = dma("gpsimd", rowb[0:1, 0:512], srow(3, 0, 512),
                        after=[tk1])
            t_k1i = dma("gpsimd", colv[:, 40:44],
                        rowb.ap()[0:1, 0:512].rearrange(
                            "a (m p) -> (a p) m", p=128).opt(),
                        after=[t_k1o], slow=True)
            t_k3o = dma("gpsimd", rowb[0:1, 512:1024], srow(3, 512, 512),
                        after=[tk3, t_k1i])
            t_k3i = dma("gpsimd", colv[:, 44:48],
                        rowb.ap()[0:1, 512:1024].rearrange(
                            "a (m p) -> (a p) m", p=128).opt(),
                        after=[t_k3o], slow=True)

            t_qn = []
            tq_prev = t_cb
            for kt in range(NKT):
                ta = op("vector",
                        lambda eng, kt=kt, c0=c0: eng.tensor_tensor(
                            scr[:, :], xtile(kt, c0, 512),
                            slot(1)[:, 0:512], ALU.mult),
                        after=[t_rsb, t_cb, tq_prev, t_wqdma2,
                               t_wqdma2b] + scr_prev)
                ta2 = op("vector",
                         lambda eng: eng.tensor_tensor(
                             scr[:, :], scr[:, :], slot(0)[:, 0:512],
                             ALU.add), after=[ta])
                tb = op("vector",
                        lambda eng: eng.tensor_copy(sci[:, :], scr[:, :]),
                        after=[ta2])
                td = op("vector",
                        lambda eng: eng.tensor_tensor(
                            scr[:, :], scr[:, :], sci[:, :], ALU.subtract),
                        after=[tb])
                te = op("vector",
                        lambda eng: eng.tensor_single_scalar(
                            scr[:, :], scr[:, :], 0.5, ALU.is_gt),
                        after=[td])
                tc_ = op("vector",
                         lambda eng, kt=kt: eng.scalar_tensor_tensor(
                             qn[:, kt * 512:(kt + 1) * 512], sci[:, :],
                             -128.0, scr[:, :], ALU.add, ALU.add),
                         after=[te])
                t_qn.append(tc_)
                tq_prev = tc_
                scr_prev = [tc_]

            t_hmax0 = []
            for mt in range(4):
                t_hmax0.append(op(
                    "vector",
                    lambda eng, mt=mt: eng.memset(
                        colv[:, 56 + mt:57 + mt], 0.0), after=dep0))
            t_a_mm, t_b_mm, t_h = {}, {}, {}
            th_prev = None
            for ch in range(NCHUNK):
                depA = [t_agw1] + ([t_a_mm[(ch - 1, 3)]] if ch else dep0)
                depB = [t_agw3] + ([t_b_mm[(ch - 1, 3)]] if ch else dep0)
                tldA = dma("sync",
                           sbuf_stream(0).rearrange("p (k c) -> p k c", k=16),
                           w1g.ap()[:, ch * CHUNK:(ch + 1) * CHUNK].rearrange(
                               "(k p) c -> p k c", p=128), after=depA)
                tldB = dma("scalar",
                           sbuf_stream(1).rearrange("p (k c) -> p k c", k=16),
                           w3g.ap()[:, ch * CHUNK:(ch + 1) * CHUNK].rearrange(
                               "(k p) c -> p k c", p=128), after=depB)
                for mt in range(4):
                    depm = t_qn + [tldA]
                    if ch >= 1:
                        depm.append(t_h[(ch - 1, mt)])
                    else:
                        depm = depm + t_lncfree + t_hrev
                    t_a_mm[(ch, mt)] = op(
                        "tensor",
                        lambda eng, mt=mt: [
                            eng.matmul(PS[mt][:, 0:CHUNK],
                                       qn[:, kt * 512 + mt * 128:
                                          kt * 512 + (mt + 1) * 128],
                                       sbuf_stream(0)[:, kt * CHUNK:
                                                      (kt + 1) * CHUNK],
                                       start=(kt == 0),
                                       stop=(kt == NKT - 1))
                            for kt in range(NKT)][-1],
                        after=depm)
                    depb = [tldB] + ([t_h[(ch - 1, mt)]] if ch else
                                     t_qn + t_lncfree)
                    t_b_mm[(ch, mt)] = op(
                        "tensor",
                        lambda eng, mt=mt: [
                            eng.matmul(PS[4 + mt][:, 0:CHUNK],
                                       qn[:, kt * 512 + mt * 128:
                                          kt * 512 + (mt + 1) * 128],
                                       sbuf_stream(1)[:, kt * CHUNK:
                                                      (kt + 1) * CHUNK],
                                       start=(kt == 0),
                                       stop=(kt == NKT - 1))
                            for kt in range(NKT)][-1],
                        after=depb)
                    tsa = op("scalar",
                             lambda eng, mt=mt: eng.activation(
                                 scr[:, 0:CHUNK], PS[mt][:, 0:CHUNK],
                                 AF.Silu, scale=colv[:, 40 + mt:41 + mt]),
                             after=[t_a_mm[(ch, mt)], t_k1i] +
                             ([th_prev] if th_prev else scr_prev))
                    tsb = op("scalar",
                             lambda eng, mt=mt: eng.activation(
                                 scr[:, CHUNK:2 * CHUNK],
                                 PS[4 + mt][:, 0:CHUNK], AF.Copy,
                                 scale=colv[:, 44 + mt:45 + mt]),
                             after=[t_b_mm[(ch, mt)], t_k3i])
                    th = op("vector",
                            lambda eng, mt=mt, ch=ch: eng.tensor_tensor(
                                raf(mt * HID + ch * CHUNK, CHUNK),
                                scr[:, 0:CHUNK], scr[:, CHUNK:2 * CHUNK],
                                ALU.mult), after=[tsa, tsb])
                    tr = op("vector",
                            lambda eng, mt=mt, ch=ch: eng.reduce_max(
                                smv[0:128, 24 + mt:25 + mt],
                                raf(mt * HID + ch * CHUNK, CHUNK), AX.X,
                                apply_absolute_value=True),
                            after=[th] + ([t_h[(ch - 1, mt)]] if ch else []))
                    t_h[(ch, mt)] = op(
                        "vector",
                        lambda eng, mt=mt: eng.tensor_tensor(
                            colv[:, 56 + mt:57 + mt],
                            colv[:, 56 + mt:57 + mt],
                            smv[0:128, 24 + mt:25 + mt], ALU.max),
                        after=[tr, t_hmax0[mt]])
                    th_prev = th
            scr_prev = [th_prev]
            t_h_all = [t_h[(NCHUNK - 1, mt)] for mt in range(4)]

            ts1 = op("vector",
                     lambda eng: eng.tensor_single_scalar(
                         colv[:, 48:52], colv[:, 56:60], Q_EPS, ALU.max),
                     after=t_h_all)
            ts2 = op("vector",
                     lambda eng: eng.reciprocal(
                         colv[:, 48:52], colv[:, 48:52]), after=[ts1])
            ts3 = op("vector",
                     lambda eng: eng.tensor_single_scalar(
                         colv[:, 48:52], colv[:, 48:52], 127.0, ALU.mult),
                     after=[ts2])
            tk2 = op("vector",
                     lambda eng: eng.reciprocal(
                         colv[:, 52:56], colv[:, 48:52]), after=[ts3])
            tk2b = op("vector",
                      lambda eng: eng.tensor_scalar(
                          colv[:, 52:56], colv[:, 52:56], smv[:, 18:19],
                          None, ALU.mult), after=[tk2])
            t_k2o = dma("gpsimd",
                        rowb.ap()[0:1, 0:512].rearrange(
                            "a (m p) -> (a p) m", p=128).opt(),
                        colv[:, 52:56], after=[tk2b, t_k3i], slow=True)
            t_k2i = dma("gpsimd", srow(2, 0, 512), rowb[0:1, 0:512],
                        after=[t_k2o, t5])
            t_k2b = bcast_row(srow(2, 0, 512), slot(2)[:, 0:512], 512,
                              5, [t_k2i] + ps5_prev + t_h_all)
            ps5_prev = [t_k2b]

            qh = ra[:, 0:11264].bitcast(BF16)
            t_qh = {}
            tqp = [ts3] + t_h_all + scr_prev
            for mt in range(4):
                for cq in range(11):
                    ta = op("vector",
                            lambda eng, mt=mt, cq=cq: eng.tensor_scalar(
                                scr[:, :], raf(mt * HID + cq * 512, 512),
                                colv[:, 48 + mt:49 + mt], 128.0,
                                ALU.mult, ALU.add), after=tqp)
                    tb = op("vector",
                            lambda eng: eng.tensor_copy(sci[:, :],
                                                        scr[:, :]),
                            after=[ta])
                    td = op("vector",
                            lambda eng: eng.tensor_tensor(
                                scr[:, :], scr[:, :], sci[:, :],
                                ALU.subtract), after=[tb])
                    te = op("vector",
                            lambda eng: eng.tensor_single_scalar(
                                scr[:, :], scr[:, :], 0.5, ALU.is_gt),
                            after=[td])
                    tf = op("vector",
                            lambda eng, mt=mt, cq=cq:
                            eng.scalar_tensor_tensor(
                                qh[:, mt * HID + cq * 512:
                                   mt * HID + (cq + 1) * 512],
                                sci[:, :], -128.0, scr[:, :],
                                ALU.add, ALU.add), after=[te])
                    t_qh[(mt, cq)] = tf
                    tqp = [tf]

            qhT = ra[:, 11264:22528].bitcast(BF16)

            def qhT_t(kt):
                return qhT[:, kt * 512:(kt + 1) * 512]

            tp_bank_free = {0: None, 1: None}
            t_tp_all = []
            t_qh_last = t_qh[(3, 10)]
            for kt in range(HKT):
                for mt in range(4):
                    bank = (kt * 4 + mt) % 2
                    dep = [t_qh_last, t_idb] + t_h_all
                    if tp_bank_free[bank] is not None:
                        dep.append(tp_bank_free[bank])
                    tmm = op(
                        "tensor",
                        lambda eng, kt=kt, mt=mt, bank=bank: eng.transpose(
                            PS[2 + bank][:, 0:512].bitcast(BF16)[:, 0:128],
                            qh[:, mt * HID + kt * 128:
                               mt * HID + (kt + 1) * 128],
                            IDB[:]),
                        after=dep)
                    tev = op("scalar",
                             lambda eng, kt=kt, mt=mt, bank=bank: eng.copy(
                                 qhT_t(kt)[:, mt * 128:(mt + 1) * 128],
                                 PS[2 + bank][:, 0:512].bitcast(
                                     BF16)[:, 0:128]),
                             after=[tmm])
                    tp_bank_free[bank] = tev
                    t_tp_all.append(tev)
            t_tp_last = [t_tp_all[-1]]

            t_g2ev = []
            g2_buf_free = {0: None, 1: None}
            for grp in range(2):
                last_mm = None
                for kt in range(HKT):
                    sb_i = kt % 2
                    depL = [t_agw2] + t_tp_last
                    if g2_buf_free[sb_i] is not None:
                        depL.append(g2_buf_free[sb_i])
                    tld = dma("sync", sbuf_stream(sb_i)[:, 0:1024],
                              w2g.ap()[kt * 128:(kt + 1) * 128,
                                       grp * 1024:(grp + 1) * 1024],
                              after=depL, semname="dL%d" % sb_i)
                    depM = [tld]
                    if kt == 0:
                        depM = depM + (t_g2ev[-8:] if grp == 1 else
                                       t_tp_last + t_lncfree + [t_k2b])
                    last_mm = op(
                        "tensor",
                        lambda eng, kt=kt, sb_i=sb_i: [
                            eng.matmul(PS[m8][:, :],
                                       sbuf_stream(sb_i)[:, m8 * 128:
                                                         (m8 + 1) * 128],
                                       qhT_t(kt),
                                       start=(kt == 0),
                                       stop=(kt == HKT - 1))
                            for m8 in range(8)][-1],
                        after=depM)
                    g2_buf_free[sb_i] = last_mm
                for m8 in range(8):
                    mtd = grp * 8 + m8
                    te1 = op("vector",
                             lambda eng, m8=m8: eng.tensor_tensor(
                                 scr[:, :], PS[m8][:, :],
                                 slot(2)[:, 0:512], ALU.mult),
                             after=[last_mm, t_k2b] +
                             (t_g2ev[-1:] if t_g2ev else scr_prev))
                    te2 = op("vector",
                             lambda eng, mtd=mtd, c0=c0: eng.tensor_tensor(
                                 xtile(mtd, c0, 512), scr[:, :],
                                 xtile(mtd, c0, 512), ALU.add),
                             after=[te1])
                    t_g2ev.append(te2)
            scr_prev = t_g2ev[-1:]
            t_half_done[hf] = t_g2ev[-1]

        for kt in range(NKT):
            dma("sync", outT[kt * 128:(kt + 1) * 128, :], xtile(kt),
                after=[t_half_done[0], t_half_done[1]])
        fin = cnt["dA"]
        steps["sync"].append(lambda eng, fin=fin: eng.wait_ge(sem["dA"], fin))

        with nc.Block() as block:
            @block.sync
            def _(eng):
                for fn in steps["sync"]:
                    fn(eng)

            @block.vector
            def _(eng):
                for fn in steps["vector"]:
                    fn(eng)

            @block.scalar
            def _(eng):
                for fn in steps["scalar"]:
                    fn(eng)

            @block.gpsimd
            def _(eng):
                for fn in steps["gpsimd"]:
                    fn(eng)

            @block.tensor
            def _(eng):
                for fn in steps["tensor"]:
                    fn(eng)

    return nc


def _chain_masks(core):
    m = np.zeros((1, 64), np.float32)
    b, s = core // 4, core % 4
    for st in range(3):
        j = b * 4 + st
        if st < s:
            m[0, st * 8 + j] = 1.0
        else:
            m[0, 24 + st] = 1.0
    for st in range(3):
        j = b * 4 + (3 - st)
        if (3 - st) > s:
            m[0, 27 + st * 8 + j] = 1.0
        else:
            m[0, 51 + st] = 1.0
    return np.repeat(m, 128, axis=0).astype(np.float32)


def kernel(L0_attn_x, L0_attn_W_in, L0_attn_W_fwd, L0_attn_W_bwd,
           L0_attn_W_out, L0_attn_gamma, L0_attn_beta, L0_ffn_w1,
           L0_ffn_w2, L0_ffn_w3):
    x = np.ascontiguousarray(np.asarray(L0_attn_x, dtype=np.float32))
    W_in = np.ascontiguousarray(np.asarray(L0_attn_W_in, np.float32))
    W_f = np.ascontiguousarray(np.asarray(L0_attn_W_fwd, np.float32))
    W_b = np.ascontiguousarray(np.asarray(L0_attn_W_bwd, np.float32))
    W_o = np.ascontiguousarray(np.asarray(L0_attn_W_out, np.float32))
    gam = np.asarray(L0_attn_gamma, np.float32).reshape(NKT, 128).T.copy()
    bet = np.asarray(L0_attn_beta, np.float32).reshape(NKT, 128).T.copy()
    w1T = np.ascontiguousarray(np.asarray(L0_ffn_w1, np.float32).T)
    w3T = np.ascontiguousarray(np.asarray(L0_ffn_w3, np.float32).T)
    w2T = np.ascontiguousarray(np.asarray(L0_ffn_w2, np.float32).T)
    ones = np.ones((128, 1), np.float32)
    identb = np.eye(128, dtype=ml_dtypes.bfloat16)
    identf = np.eye(128, dtype=np.float32)
    onesrow = np.ones((1, 128), np.float32)

    nc = build_nc()
    in_maps = []
    for core in range(8):
        b, s = core // 4, core % 4
        xs = np.ascontiguousarray(x[b, s * T:(s + 1) * T, :].T)
        in_maps.append({
            "xT": xs,
            "W_in": W_in, "W_f": W_f, "W_b": W_b, "W_o": W_o,
            "gamP": gam, "betP": bet,
            "w1s": np.ascontiguousarray(w1T[core * 256:(core + 1) * 256, :]),
            "w3s": np.ascontiguousarray(w3T[core * 256:(core + 1) * 256, :]),
            "w2s": np.ascontiguousarray(w2T[core * 704:(core + 1) * 704, :]),
            "cmask": _chain_masks(core),
            "ones": ones,
            "identb": identb,
            "identf": identf,
            "onesrow": onesrow,
        })
    res = run_bass_kernel_spmd(nc, in_maps, core_ids=list(range(8)))
    out = np.empty((B, L, D), np.float32)
    for core in range(8):
        b, s = core // 4, core % 4
        out[b, s * T:(s + 1) * T, :] = res.results[core]["outT"].T
    return out


# version 27
# speedup vs baseline: 1.0634x; 1.0042x over previous
import numpy as np
import ml_dtypes
from contextlib import ExitStack

import concourse.bass as bass
import concourse.mybir as mybir
from concourse.bass_utils import run_bass_kernel_spmd

F32 = mybir.dt.float32
BF16 = mybir.dt.bfloat16
I32 = mybir.dt.int32
AF = mybir.ActivationFunctionType
ALU = mybir.AluOpType
AX = mybir.AxisListType

B, L, D, DI = 2, 4096, 2048, 256
HID = 5632
T = 1024
NKT = D // 128
LN_EPS = 1e-5
Q_EPS = 1e-5
NW1 = float(HID * D)
NW2 = float(D * HID)
CHUNK = 256
NCHUNK = HID // CHUNK
HKT = HID // 128
RG = [list(range(8))]


def build_nc():
    nc = bass.Bass("TRN2", target_bir_lowering=False, debug=False)

    xT = nc.dram_tensor("xT", [D, T], F32, kind="ExternalInput").ap()
    W_in = nc.dram_tensor("W_in", [D, DI], F32, kind="ExternalInput").ap()
    W_f = nc.dram_tensor("W_f", [DI, 2 * DI], F32, kind="ExternalInput").ap()
    W_b = nc.dram_tensor("W_b", [DI, 2 * DI], F32, kind="ExternalInput").ap()
    W_o = nc.dram_tensor("W_o", [2 * DI, D], F32, kind="ExternalInput").ap()
    gamP = nc.dram_tensor("gamP", [128, NKT], F32, kind="ExternalInput").ap()
    betP = nc.dram_tensor("betP", [128, NKT], F32, kind="ExternalInput").ap()
    w1s = nc.dram_tensor("w1s", [D // 8, HID], F32, kind="ExternalInput").ap()
    w3s = nc.dram_tensor("w3s", [D // 8, HID], F32, kind="ExternalInput").ap()
    w2s = nc.dram_tensor("w2s", [HID // 8, D], F32, kind="ExternalInput").ap()
    cmask = nc.dram_tensor("cmask", [128, 64], F32, kind="ExternalInput").ap()
    ones = nc.dram_tensor("ones", [128, 1], F32, kind="ExternalInput").ap()
    identb = nc.dram_tensor("identb", [128, 128], BF16,
                            kind="ExternalInput").ap()
    identf = nc.dram_tensor("identf", [128, 128], F32,
                            kind="ExternalInput").ap()
    onesrow = nc.dram_tensor("onesrow", [1, 128], F32,
                             kind="ExternalInput").ap()

    outT = nc.dram_tensor("outT", [D, T], F32, kind="ExternalOutput").ap()

    ag_sums_in = nc.dram_tensor("ag_sums_in", [1, 4], F32)
    ag_sums_out = nc.dram_tensor("ag_sums_out", [8, 4], F32)
    ag_scan_in = nc.dram_tensor("ag_scan_in", [128, 8], F32)
    ag_scan_out = nc.dram_tensor("ag_scan_out", [1024, 8], F32)
    ag_w1_in = nc.dram_tensor("ag_w1_in", [D // 8, HID], BF16)
    w1g = nc.dram_tensor("w1g", [D, HID], BF16)
    ag_w3_in = nc.dram_tensor("ag_w3_in", [D // 8, HID], BF16)
    w3g = nc.dram_tensor("w3g", [D, HID], BF16)
    ag_w2_in = nc.dram_tensor("ag_w2_in", [HID // 8, D], BF16)
    w2g = nc.dram_tensor("w2g", [HID, D], BF16)
    rowb = nc.dram_tensor("rowb", [1, T], F32)

    es = ExitStack()
    with es:
        XB = es.enter_context(nc.sbuf_tensor("XB", [128, NKT * T], F32))
        RA = es.enter_context(nc.sbuf_tensor("RA", [128, 22528], F32))
        QN = es.enter_context(nc.sbuf_tensor("QN", [128, 8192], BF16))
        SH = es.enter_context(nc.sbuf_tensor("SH", [128, 8192], F32))
        SCR = es.enter_context(nc.sbuf_tensor("SCR", [128, 512], F32))
        SCI = es.enter_context(nc.sbuf_tensor("SCI", [128, 512], I32))
        COLS = es.enter_context(nc.sbuf_tensor("COLS", [128, 64], F32))
        GATH = es.enter_context(nc.sbuf_tensor("GATH", [128, 80], F32))
        MASK = es.enter_context(nc.sbuf_tensor("MASK", [128, 64], F32))
        SMALL = es.enter_context(nc.sbuf_tensor("SMALL", [128, 32], F32))
        ONES = es.enter_context(nc.sbuf_tensor("ONES", [128, 1], F32))
        IDB = es.enter_context(nc.sbuf_tensor("IDB", [128, 128], BF16))
        IDF = es.enter_context(nc.sbuf_tensor("IDF", [128, 128], F32))
        ONESR = es.enter_context(nc.sbuf_tensor("ONESR", [1, 128], F32))
        PS = [es.enter_context(nc.psum_tensor(f"PS{i}", [128, 512], F32))
              for i in range(8)]
        sem = {n: es.enter_context(nc.semaphore(n))
               for n in ["dA", "dG", "dS", "TS", "VS", "SS", "GS", "CC",
                         "dW0", "dW1", "dT0", "dT1", "dL0", "dL1", "dX",
                         "dV0", "dV1", "dU0", "dU1", "dE", "dO1", "dO2"]}

        steps = {"sync": [], "vector": [], "scalar": [], "gpsimd": [],
                 "tensor": []}
        cnt = {}

        def bump(nm, amt=1):
            cnt[nm] = cnt.get(nm, 0) + amt
            return (nm, cnt[nm])

        QSEM = {"sync": "dA", "gpsimd": "dG", "scalar": "dS",
                "vector": "dE"}
        ESEM = {"vector": "VS", "scalar": "SS", "tensor": "TS",
                "gpsimd": "GS"}

        def dma(queue, out, in_, after=(), slow=False, semname=None):
            tok = bump(semname or QSEM[queue], 16)

            def fn(eng, out=out, in_=in_, after=tuple(after), tok=tok):
                for nm, v in after:
                    eng.wait_ge(sem[nm], v)
                eng.dma_start(out=out, in_=in_,
                              allow_slow_non_contiguous=slow
                              ).then_inc(sem[tok[0]], 16)
            steps[queue].append(fn)
            return tok

        def op(engine, fn, after=()):
            tok = bump(ESEM[engine])

            def wrapped(eng, fn=fn, after=tuple(after), tok=tok):
                for nm, v in after:
                    eng.wait_ge(sem[nm], v)
                fn(eng).then_inc(sem[tok[0]], 1)
            steps[engine].append(wrapped)
            return tok

        def cc(ins_h, outs_h, after=()):
            tok = bump("CC")

            def fn(eng, tok=tok, after=tuple(after)):
                for nm, v in after:
                    eng.wait_ge(sem[nm], v)
                eng.collective_compute(
                    "AllGather", ALU.bypass, replica_groups=RG,
                    ins=[ins_h.ap().opt()], outs=[outs_h.ap().opt()]
                ).then_inc(sem["CC"], 1)
            steps["gpsimd"].append(fn)
            return tok

        ra, shp, xb, qn = RA.ap(), SH.ap(), XB.ap(), QN.ap()
        colv, smv, gat, msk = COLS.ap(), SMALL.ap(), GATH.ap(), MASK.ap()
        scr, sci = SCR.ap(), SCI.ap()

        def raf(a, n):
            return ra[:, a:a + n]

        def rabf(a, n):
            return ra[:, a:a + (n + 1) // 2].bitcast(BF16)[:, 0:n]

        def rai32(a, n):
            return ra[:, a:a + n].bitcast(I32)[:, 0:n]

        def slot(i):
            return shp[:, i * T:(i + 1) * T]

        def srow(i, a=0, n=T):
            return shp[0:1, i * T + a:i * T + a + n]

        def sbuf_stream(i):
            return shp[:, 4096 + i * 2048:4096 + (i + 1) * 2048].bitcast(
                BF16)[:, 0:4096]

        def xtile(kt, c0=0, cn=T):
            return xb[:, kt * T + c0:kt * T + c0 + cn]

        O_WIN, O_WFB, O_Z, O_ZR, O_DG, O_P, O_H = (
            0, 4096, 6144, 8192, 10240, 14336, 18432)
        O_Y = 8192
        O_HR = 16384
        qnf = qn[:, 0:8192].bitcast(F32)

        def wqa(i):
            return qnf[:, i * 1024:(i + 1) * 1024]

        wqf2 = qnf[:, 2048:3072]
        wqf3 = qnf[:, 3072:4096]

        def wqb(i):
            return shp[:, 6144 + i * 512:6144 + (i + 1) * 512].bitcast(
                BF16)[:, 0:1024]

        dma("sync", ONES[:], ones)
        dma("sync", colv[:, 0:NKT], gamP)
        dma("sync", colv[:, NKT:2 * NKT], betP)
        dma("sync", msk[:, :], cmask)
        dma("sync", IDB[:], identb)
        dma("sync", IDF[:], identf)
        dma("sync", ONESR[:], onesrow)
        for kt in range(8):
            dma("sync", xtile(kt), xT[kt * 128:(kt + 1) * 128, :])
        tBAT1 = ("dA", cnt["dA"])
        for kt in range(8, NKT):
            dma("scalar", xtile(kt), xT[kt * 128:(kt + 1) * 128, :],
                semname="dE")
        tBATX = ("dE", cnt["dE"])
        dma("sync", raf(O_WIN, 4096).rearrange("p (k c) -> p k c", k=16),
            W_in.rearrange("(k p) c -> p k c", p=128), after=[tBAT1])
        dma("sync", raf(O_WFB, 1024).rearrange("p (k c) -> p k c", k=2),
            W_f.rearrange("(k p) c -> p k c", p=128))
        dma("sync", raf(O_WFB + 1024, 1024).rearrange("p (k c) -> p k c", k=2),
            W_b.rearrange("(k p) c -> p k c", p=128))
        tINIT2 = ("dA", cnt["dA"])
        t_x = [tBAT1, tBATX]
        t_ones = t_gam = t_bet = t_mask = t_idb = t_idf = t_onr = tBAT1
        t_win = t_wf = t_wb = tINIT2

        def wchunks(ncols):
            out, a = [], 0
            while a < ncols:
                n = min(1024, ncols - a)
                out.append((a, n))
                a += n
            return out

        wq_buf_free = {}

        def wsum_pass(wap, nrows, ncols, acc_col, queue, semn, bufn, bufs,
                      prev=()):
            acc = smv[:, acc_col:acc_col + 1]
            tred = op("vector", lambda eng: eng.memset(acc, 0.0),
                      after=list(prev))
            ci = 0
            for pt in range((nrows + 127) // 128):
                pr = min(128, nrows - pt * 128)
                for (a, n) in wchunks(ncols):
                    bi = ci % 2
                    key = bufn + str(bi)
                    dep = ([wq_buf_free[key]] if wq_buf_free.get(key)
                           else [])
                    tl = dma(queue, bufs[bi][0:pr, 0:n],
                             wap[pt * 128:pt * 128 + pr, a:a + n], after=dep,
                             semname="%s%d" % (semn, bi))
                    tr = op("vector",
                            lambda eng, pr=pr, n=n, bi=bi, bufs=bufs:
                            eng.reduce_sum(
                                scr[0:pr, 0:1], bufs[bi][0:pr, 0:n], AX.X,
                                apply_absolute_value=True),
                            after=[tl, tred])
                    tred = op("vector",
                              lambda eng, pr=pr: eng.tensor_tensor(
                                  acc[0:pr, :], acc[0:pr, :],
                                  scr[0:pr, 0:1], ALU.add),
                              after=[tr, tred])
                    wq_buf_free[key] = tr
                    ci += 1
            return tred

        shbuf = [shp[:, 4096:5120], shp[:, 5120:6144]]
        t_r1_ = wsum_pass(w1s, 256, HID, 0, "gpsimd", "dW", "W",
                          [wqa(0), wqa(1)])
        t_r3_ = wsum_pass(w3s, 256, HID, 1, "sync", "dV", "V",
                          [wqa(2), wqa(3)], prev=[t_r1_])
        t_r2_ = wsum_pass(w2s, 704, D, 2, "scalar", "dU", "U", shbuf,
                          prev=[t_r3_])

        def ln_stats(src, ncols, sq_scratch, mu_row, r_row, scratch_row,
                     after=()):
            nnc = ncols // 512
            after = list(after)
            tmm = op("tensor",
                     lambda eng: [
                         eng.matmul(PS[4 + nc_][0:1, :], ONES[:],
                                    src(kt)[:, nc_ * 512:(nc_ + 1) * 512],
                                    start=(kt == 0), stop=(kt == NKT - 1))
                         for nc_ in range(nnc) for kt in range(NKT)][-1],
                     after=after + [t_ones])
            tq_mm = None
            for kt in range(NKT):
                tS = op("scalar",
                        lambda eng, kt=kt: eng.activation(
                            sq_scratch, src(kt), AF.Square),
                        after=after + ([tq_mm] if tq_mm else []))
                tq_mm = op("tensor",
                           lambda eng, kt=kt: [
                               eng.matmul(
                                   PS[6 + nc_][0:1, :], ONES[:],
                                   sq_scratch[:, nc_ * 512:(nc_ + 1) * 512],
                                   start=(kt == 0), stop=(kt == NKT - 1))
                               for nc_ in range(nnc)][-1],
                           after=[tS])
            tmu = []
            tms = []
            for nc_ in range(nnc):
                tmu.append(op(
                    "vector",
                    lambda eng, nc_=nc_: eng.tensor_scalar(
                        mu_row[0:1, nc_ * 512:(nc_ + 1) * 512],
                        PS[4 + nc_][0:1, :], 1.0 / D, None, ALU.mult),
                    after=[tmm]))
                tms.append(op(
                    "vector",
                    lambda eng, nc_=nc_: eng.tensor_scalar(
                        scratch_row[0:1, nc_ * 512:(nc_ + 1) * 512],
                        PS[6 + nc_][0:1, :], 1.0 / D, None, ALU.mult),
                    after=[tq_mm]))
            tv1 = op("vector",
                     lambda eng: eng.tensor_tensor(
                         r_row, mu_row, mu_row, ALU.mult), after=tmu)
            tv2 = op("vector",
                     lambda eng: eng.tensor_tensor(
                         scratch_row, scratch_row, r_row, ALU.subtract),
                     after=tms + [tv1])
            tv3 = op("vector",
                     lambda eng: eng.tensor_single_scalar(
                         scratch_row, scratch_row, LN_EPS, ALU.add),
                     after=[tv2])
            tl1 = op("scalar",
                     lambda eng: eng.activation(r_row, scratch_row, AF.Ln),
                     after=[tv3])
            tl2 = op("scalar",
                     lambda eng: eng.activation(r_row, r_row, AF.Exp,
                                                scale=-0.5), after=[tl1])
            tfree = tmu + tms
            return tl2, tfree

        t_r1, t_ln1free = ln_stats(lambda kt: xtile(kt), 1024, slot(2),
                                   srow(0), srow(1), srow(2), after=t_x)
        t_mur1 = op("vector",
                    lambda eng: eng.tensor_tensor(
                        srow(2), srow(0), srow(1), ALU.mult), after=[t_r1])
        def bcast_row(row_ap, dst_ap, n, bank, after):
            tmm = op("tensor",
                     lambda eng: eng.matmul(PS[bank][:, 0:n], ONESR[:],
                                            row_ap, start=True, stop=True),
                     after=list(after) + [t_onr])
            return op("scalar",
                      lambda eng: eng.copy(dst_ap, PS[bank][:, 0:n]),
                      after=[tmm])

        tb0 = bcast_row(srow(1, 0, 512), slot(0)[:, 0:512], 512, 2,
                        [t_r1, t_mur1])
        t_r1b = bcast_row(srow(1, 512, 512), slot(0)[:, 512:1024], 512, 3,
                          [t_r1, tb0])
        tb1 = bcast_row(srow(2, 0, 512), slot(1)[:, 0:512], 512, 2,
                        [t_mur1, t_r1b])
        t_mur1b = bcast_row(srow(2, 512, 512), slot(1)[:, 512:1024], 512, 3,
                            [t_mur1, tb1])

        def u1t(i):
            return raf(O_DG + i * 1024, 1024)

        t_u1_mm = {}
        for kt in range(NKT):
            buf = kt % 2
            dep = [t_r1b, t_mur1b]
            if kt >= 2:
                dep.append(t_u1_mm[kt - 2])
            ta = op("vector",
                    lambda eng, kt=kt, buf=buf: eng.tensor_tensor(
                        u1t(buf), xtile(kt), slot(0), ALU.mult), after=dep)
            tb = op("vector",
                    lambda eng, buf=buf: eng.tensor_tensor(
                        u1t(buf), u1t(buf), slot(1), ALU.subtract),
                    after=[ta])
            t_u1_mm[kt] = op(
                "tensor",
                lambda eng, kt=kt, buf=buf: [
                    eng.matmul(PS[mt * 2 + nc_][:, :],
                               raf(O_WIN + kt * 256 + mt * 128, 128),
                               u1t(buf)[:, nc_ * 512:(nc_ + 1) * 512],
                               start=(kt == 0), stop=(kt == NKT - 1))
                    for mt in range(2) for nc_ in range(2)][-1],
                after=[tb, t_win] + ([] if kt else [t_r1b, t_mur1b]))
        t_zev = []
        for mt in range(2):
            for nc_ in range(2):
                t_zev.append(op(
                    "scalar",
                    lambda eng, mt=mt, nc_=nc_: eng.copy(
                        raf(O_Z + mt * 1024 + nc_ * 512, 512),
                        PS[mt * 2 + nc_][:, :]),
                    after=[t_u1_mm[NKT - 1]]))

        t_zrev = []
        for g in range(2):
            t_zrev.append(op(
                "vector",
                lambda eng, g=g: eng.tensor_copy(
                    raf(O_ZR + g * 1024, 1024),
                    raf(O_Z + g * 1024, 1024)[:, ::-1]),
                after=t_zev))

        t_zero = op("vector",
                    lambda eng: eng.memset(raf(O_WIN + 1024, 1024), 0.0),
                    after=[t_u1_mm[NKT - 1]])
        zeros1024 = raf(O_WIN + 1024, 1024)

        t_smm = op("tensor",
                   lambda eng: eng.matmul(PS[7][0:1, 0:3], ONES[:],
                                          smv[:, 0:3], start=True, stop=True),
                   after=[t_r1_, t_r3_, t_r2_, t_ones] + t_ln1free)
        t_s2 = op("scalar",
                  lambda eng: eng.copy(smv[0:1, 0:3], PS[7][0:1, 0:3]),
                  after=[t_smm])
        t_z3 = op("vector", lambda eng: eng.memset(smv[0:1, 3:4], 0.0))
        t_dsum = dma("gpsimd", ag_sums_in[0:1, 0:4], smv[0:1, 0:4],
                     after=[t_s2, t_z3])
        t_ccs = cc(ag_sums_in, ag_sums_out, after=[t_dsum])
        t_dsb = dma("gpsimd", smv[0:8, 8:12], ag_sums_out[:, :],
                    after=[t_ccs])
        t_tot = op("tensor",
                   lambda eng: eng.matmul(PS[6][0:1, 0:4], ONES[0:8, :],
                                          smv[0:8, 8:12], start=True,
                                          stop=True),
                   after=[t_dsb, t_ones] + t_ln1free)
        t_totr = op("scalar",
                    lambda eng: eng.copy(smv[0:1, 8:12], PS[6][0:1, 0:4]),
                    after=[t_tot])
        t_bcm = op("tensor",
                   lambda eng: eng.matmul(PS[6][:, 0:4], ONESR[:],
                                          smv[0:1, 8:12], start=True,
                                          stop=True),
                   after=[t_totr, t_onr])
        t_t128 = op("scalar",
                    lambda eng: eng.copy(smv[:, 12:16], PS[6][:, 0:4]),
                    after=[t_bcm])
        t_mwa = op("vector",
                   lambda eng: eng.tensor_scalar(
                       smv[:, 16:17], smv[:, 12:13], 1.0 / NW1, Q_EPS,
                       ALU.mult, ALU.max), after=[t_t128])
        t_mwb = op("vector",
                   lambda eng: eng.tensor_scalar(
                       smv[:, 17:18], smv[:, 13:14], 1.0 / NW1, Q_EPS,
                       ALU.mult, ALU.max), after=[t_t128])
        t_mwc = op("vector",
                   lambda eng: eng.tensor_scalar(
                       smv[:, 18:19], smv[:, 14:15], 1.0 / NW2, Q_EPS,
                       ALU.mult, ALU.max), after=[t_t128])
        t_wsc = op("vector",
                   lambda eng: eng.tensor_single_scalar(
                       smv[:, 20:23], smv[:, 16:19], 0.5, ALU.mult),
                   after=[t_mwa, t_mwb, t_mwc])

        wqb_free = {}

        def wquant_pass(wap, nrows, ncols, scol, ag_in_h, ag_out_h,
                        defer_cc=False):
            tdma_last = None
            tq3_prev = None
            ci = 0
            for pt in range((nrows + 127) // 128):
                pr = min(128, nrows - pt * 128)
                for (a, n) in wchunks(ncols):
                    bi = ci % 2
                    key = "W%d" % bi
                    dep = [t_wsc] + ([wq_buf_free[key]]
                                     if wq_buf_free.get(key) else [])
                    tl = dma("sync" if bi == 0 else "scalar",
                             wqa(bi)[0:pr, 0:n],
                             wap[pt * 128:pt * 128 + pr, a:a + n], after=dep,
                             semname="dW%d" % bi)
                    tq1 = op("scalar",
                             lambda eng, pr=pr, n=n, bi=bi:
                             eng.activation(
                                 wqf2[0:pr, 0:n], wqa(bi)[0:pr, 0:n],
                                 AF.Abs),
                             after=[tl, t_wsc] +
                             ([tq3_prev] if tq3_prev else []))
                    tq2 = op("scalar",
                             lambda eng, pr=pr, n=n, bi=bi:
                             eng.activation(
                                 wqf3[0:pr, 0:n], wqa(bi)[0:pr, 0:n],
                                 AF.Sign), after=[tq1])
                    tq2v = op("vector",
                              lambda eng, pr=pr, n=n, scol=scol:
                              eng.tensor_scalar(
                                  wqf2[0:pr, 0:n], wqf2[0:pr, 0:n],
                                  smv[0:pr, 20 + scol:21 + scol], None,
                                  ALU.is_gt),
                              after=[tq1] +
                              ([tq3_prev] if tq3_prev else []))
                    tq3 = op("vector",
                             lambda eng, pr=pr, n=n, bi=bi: eng.tensor_tensor(
                                 wqb(bi)[0:pr, 0:n], wqf2[0:pr, 0:n],
                                 wqf3[0:pr, 0:n], ALU.mult),
                             after=[tq2v, tq2] +
                             ([wqb_free[bi]] if wqb_free.get(bi) else []))
                    tdma_last = dma("sync" if bi == 0 else "scalar",
                                    ag_in_h[pt * 128:pt * 128 + pr, a:a + n],
                                    wqb(bi)[0:pr, 0:n], after=[tq3],
                                    semname="dT%d" % bi)
                    wq_buf_free["W%d" % bi] = tq2
                    wqb_free[bi] = tdma_last
                    tq3_prev = tq3
                    ci += 1
            tboth = [("dT0", cnt.get("dT0", 0)), ("dT1", cnt.get("dT1", 0))]
            if defer_cc:
                return tboth, tboth[0]
            return cc(ag_in_h, ag_out_h, after=tboth), tboth[0]



        t_scan_done = []
        t_extr = []
        prev_dg_ev = []
        for dire in range(2):
            zin = raf(O_Z, 2048) if dire == 0 else raf(O_ZR, 2048)
            wofs = O_WFB + dire * 1024
            tw = t_wf if dire == 0 else t_wb
            t_dg = []
            for nc_ in range(2):
                dep_mm = [tw] + t_zev + t_zrev + prev_dg_ev + t_dg
                tmm = op(
                    "tensor",
                    lambda eng, nc_=nc_, zin=zin, wofs=wofs: [
                        eng.matmul(PS[mt][:, :],
                                   raf(wofs + kt * 512 + mt * 128, 128),
                                   zin[:, kt * 1024 + nc_ * 512:
                                       kt * 1024 + (nc_ + 1) * 512],
                                   start=(kt == 0), stop=(kt == 1))
                        for mt in range(4) for kt in range(2)][-1],
                    after=dep_mm)
                for mt in range(2):
                    t_dg.append(op(
                        "scalar",
                        lambda eng, mt=mt, nc_=nc_: eng.activation(
                            raf(O_DG + mt * 1024 + nc_ * 512, 512),
                            PS[mt][:, :], AF.Sigmoid),
                        after=[tmm] + t_scan_done))
                for mt in range(2, 4):
                    t_dg.append(op(
                        "vector",
                        lambda eng, mt=mt, nc_=nc_, zin=zin:
                        eng.tensor_tensor(
                            raf(O_DG + 2048 + (mt - 2) * 1024 + nc_ * 512,
                                512),
                            PS[mt][:, :],
                            zin[:, (mt - 2) * 1024 + nc_ * 512:
                                (mt - 2) * 1024 + (nc_ + 1) * 512],
                            ALU.mult),
                        after=[tmm] + t_scan_done))
            prev_dg_ev = list(t_dg)
            t_sc = []
            for g in range(2):
                extra = []
                t_sc.append(op(
                    "vector",
                    lambda eng, g=g, dire=dire: eng.tensor_tensor_scan(
                        raf(O_H + dire * 2048 + g * 1024, 1024),
                        raf(O_DG + g * 1024, 1024),
                        raf(O_DG + 2048 + g * 1024, 1024),
                        0.0, ALU.mult, ALU.add),
                    after=t_dg + extra))
                t_sc.append(op(
                    "vector",
                    lambda eng, g=g, dire=dire: eng.tensor_tensor_scan(
                        raf(O_P + dire * 2048 + g * 1024, 1024),
                        raf(O_DG + g * 1024, 1024),
                        zeros1024, 1.0, ALU.mult, ALU.add),
                    after=t_dg + [t_zero] + extra))
            for g in range(2):
                t_extr.append(op(
                    "vector",
                    lambda eng, g=g, dire=dire: eng.tensor_copy(
                        gat[:, dire * 4 + 2 * g:dire * 4 + 2 * g + 1],
                        raf(O_P + dire * 2048 + g * 1024 + 1023, 1)),
                    after=t_sc))
                t_extr.append(op(
                    "vector",
                    lambda eng, g=g, dire=dire: eng.tensor_copy(
                        gat[:, dire * 4 + 2 * g + 1:dire * 4 + 2 * g + 2],
                        raf(O_H + dire * 2048 + g * 1024 + 1023, 1)),
                    after=t_sc))
            t_scan_done = t_scan_done + t_sc

        w1deps, _ = wquant_pass(w1s, 256, HID, 0, ag_w1_in, w1g,
                                defer_cc=True)
        w3deps, _ = wquant_pass(w3s, 256, HID, 1, ag_w3_in, w3g,
                                defer_cc=True)
        t_dsc = dma("sync", ag_scan_in[:, :], gat[:, 0:8], after=t_extr,
                    semname="dX")
        t_ccscan = cc(ag_scan_in, ag_scan_out, after=[t_dsc])
        t_dscb = dma("sync", gat[:, 16:80].rearrange("p (c q) -> p c q", c=8),
                     ag_scan_out.ap().rearrange("(c p) q -> p c q", p=128),
                     after=[t_ccscan], semname="dX")
        t_agw1 = cc(ag_w1_in, w1g, after=w1deps)
        t_agw3 = cc(ag_w3_in, w3g, after=w3deps)

        t_J = {}
        for dire in range(2):
            J = smv[:, 26 + dire * 2:28 + dire * 2]
            A = gat[:, 8:10]
            Ssel = gat[:, 10:12]
            tj = op("vector", lambda eng, J=J: eng.memset(J, 0.0),
                    after=[t_dscb, t_wsc] + ([t_J[0]] if dire else []))
            mbase = dire * 27
            for s in range(3):
                ta = op("vector",
                        lambda eng, s=s, mbase=mbase, A=A: eng.tensor_copy(
                            A[:, 0:1],
                            msk[:, mbase + 24 + s:mbase + 25 + s]),
                        after=[tj, t_mask])
                ta = op("vector",
                        lambda eng, s=s, mbase=mbase, A=A: eng.tensor_copy(
                            A[:, 1:2],
                            msk[:, mbase + 24 + s:mbase + 25 + s]),
                        after=[ta])
                ts_ = op("vector",
                         lambda eng, Ssel=Ssel: eng.memset(Ssel, 0.0),
                         after=[tj])
                for j in range(8):
                    q0 = 16 + j * 8 + dire * 4
                    ta = op("vector",
                            lambda eng, q0=q0, s=s, j=j, mbase=mbase, A=A:
                            eng.scalar_tensor_tensor(
                                A, gat[:, q0:q0 + 3:2],
                                msk[:, mbase + s * 8 + j:
                                    mbase + s * 8 + j + 1],
                                A, ALU.mult, ALU.add), after=[ta])
                    ts_ = op("vector",
                             lambda eng, q0=q0, s=s, j=j, mbase=mbase,
                             Ssel=Ssel: eng.scalar_tensor_tensor(
                                 Ssel, gat[:, q0 + 1:q0 + 4:2],
                                 msk[:, mbase + s * 8 + j:
                                     mbase + s * 8 + j + 1],
                                 Ssel, ALU.mult, ALU.add), after=[ts_])
                tj = op("vector",
                        lambda eng, J=J, A=A: eng.tensor_tensor(
                            J, J, A, ALU.mult), after=[ta, tj])
                tj = op("vector",
                        lambda eng, J=J, Ssel=Ssel: eng.tensor_tensor(
                            J, J, Ssel, ALU.add), after=[ts_, tj])
            t_J[dire] = tj

        t_hfix = []
        for dire in range(2):
            for g in range(2):
                t_hfix.append(op(
                    "vector",
                    lambda eng, dire=dire, g=g: eng.scalar_tensor_tensor(
                        raf(O_H + dire * 2048 + g * 1024, 1024),
                        raf(O_P + dire * 2048 + g * 1024, 1024),
                        smv[:, 26 + dire * 2 + g:27 + dire * 2 + g],
                        raf(O_H + dire * 2048 + g * 1024, 1024),
                        ALU.mult, ALU.add),
                    after=[t_J[dire]] + t_scan_done))
        t_hrev = []
        for g in range(2):
            t_hrev.append(op(
                "vector",
                lambda eng, g=g: eng.tensor_copy(
                    raf(O_HR + g * 1024, 1024),
                    raf(O_H + 2048 + g * 1024, 1024)[:, ::-1]),
                after=t_hfix))

        def hgrp(kt):
            return raf(O_H + kt * 1024, 1024) if kt < 2 else \
                raf(O_HR + (kt - 2) * 1024, 1024)

        t_wo = dma("sync",
                   raf(O_WIN, 8192).rearrange("p (k c) -> p k c", k=4),
                   W_o.rearrange("(k p) c -> p k c", p=128),
                   after=t_scan_done + t_zrev)

        def make_yprep(hf, extra):
            c0 = hf * 512
            vp, yp = None, {}
            for mtd in range(NKT):
                depv = [t_r1b, t_mur1b] + (t_hfix if mtd >= 12
                                           else t_scan_done) + list(extra)
                if vp:
                    depv.append(vp)
                v1 = op("vector",
                        lambda eng, mtd=mtd, c0=c0: eng.tensor_tensor(
                            raf(O_Y + mtd * 512, 512), xtile(mtd, c0, 512),
                            slot(0)[:, c0:c0 + 512], ALU.mult), after=depv)
                v2 = op("vector",
                        lambda eng, mtd=mtd, c0=c0: eng.tensor_tensor(
                            raf(O_Y + mtd * 512, 512),
                            raf(O_Y + mtd * 512, 512),
                            slot(1)[:, c0:c0 + 512], ALU.subtract),
                        after=[v1])
                yp[mtd] = v2
                vp = v2
            return yp

        yprep0 = make_yprep(0, [])

        t_x1 = {}
        prev_bank_ev = {0: None, 1: None}
        ps5_prev = t_ln1free
        scr_prev = [t_wsc]
        for hf in range(2):
            c0 = hf * 512
            t_yprep = yprep0 if hf == 0 else make_yprep(1, [t_x1[0]])
            t_yev = []
            for mtd in range(NKT):
                bank = mtd % 2
                dep = [t_wo, t_idf, t_yprep[mtd]] + t_hrev + t_hfix
                if prev_bank_ev[bank] is not None:
                    dep.append(prev_bank_ev[bank])
                tmm = op(
                    "tensor",
                    lambda eng, mtd=mtd, bank=bank, c0=c0: ([
                        eng.matmul(PS[bank][:, :],
                                   raf(O_WIN + kt * 2048 + mtd * 128, 128),
                                   hgrp(kt)[:, c0:c0 + 512],
                                   start=(kt == 0), stop=False)
                        for kt in range(4)] and
                        eng.matmul(PS[bank][:, :], IDF[:],
                                   raf(O_Y + mtd * 512, 512),
                                   start=False, stop=True)),
                    after=dep)
                te3 = op("scalar",
                         lambda eng, mtd=mtd, bank=bank: eng.copy(
                             raf(O_Y + mtd * 512, 512), PS[bank][:, :]),
                         after=[tmm])
                prev_bank_ev[bank] = te3
                t_yev.append(te3)

            def ysrc(kt):
                return raf(O_Y + kt * 512, 512)

            t_rby, t_lnbfree = ln_stats(
                ysrc, 512, slot(2)[:, 0:512], scr[0:1, 0:512],
                srow(2, 0, 512), srow(3, 0, 512),
                after=t_yev + scr_prev + t_ln1free)
            scr_prev = [t_lnbfree[0]]
            t_mury = op("vector",
                        lambda eng: eng.tensor_tensor(
                            scr[0:1, 0:512], scr[0:1, 0:512],
                            srow(2, 0, 512), ALU.mult), after=[t_rby])
            t_ryb = bcast_row(srow(2, 0, 512), slot(3)[:, 0:512], 512,
                              5, [t_rby] + ps5_prev)
            t_muryb = bcast_row(scr[0:1, 0:512], slot(2)[:, 0:512], 512,
                                5, [t_mury, t_ryb])
            ps5_prev = [t_muryb]
            tprev = t_muryb
            for kt in range(NKT):
                ta = op("vector",
                        lambda eng, kt=kt: eng.tensor_tensor(
                            scr[:, :], raf(O_Y + kt * 512, 512),
                            slot(3)[:, 0:512], ALU.mult),
                        after=[t_ryb, t_muryb, t_gam, t_bet, tprev,
                               t_mury])
                tb = op("vector",
                        lambda eng: eng.tensor_tensor(
                            scr[:, :], scr[:, :], slot(2)[:, 0:512],
                            ALU.subtract), after=[ta])
                tc_ = op("vector",
                         lambda eng, kt=kt, c0=c0: eng.scalar_tensor_tensor(
                             xtile(kt, c0, 512), scr[:, :],
                             colv[:, kt:kt + 1], xtile(kt, c0, 512),
                             ALU.mult, ALU.add), after=[tb])
                td = op("vector",
                        lambda eng, kt=kt, c0=c0: eng.tensor_scalar(
                            xtile(kt, c0, 512), xtile(kt, c0, 512),
                            colv[:, NKT + kt:NKT + kt + 1], None, ALU.add),
                        after=[tc_])
                tprev = td
            scr_prev = [tprev]
            t_x1[hf] = tprev

        t_agw2, t_wqdma2 = wquant_pass(w2s, 704, D, 2, ag_w2_in, w2g)
        t_wqdma2b = ("dT1", cnt.get("dT1", 0))

        t_half_done = {}
        t_out_ready = {}
        for hf in range(2):
            c0 = hf * 512
            dep0 = [t_x1[0], t_x1[1]]
            if hf == 1:
                dep0 = dep0 + [t_half_done[0]]

            def x1src(kt, c0=c0):
                return xtile(kt, c0, 512)

            t_r2, t_lncfree = ln_stats(
                x1src, 512, slot(2)[:, 0:512], scr[0:1, 0:512],
                srow(0, 0, 512), srow(1, 0, 512), after=dep0 + scr_prev)
            t_mu2b = bcast_row(scr[0:1, 0:512], slot(1)[:, 0:512], 512,
                               5, [t_r2] + ps5_prev)
            ps5_prev = [t_mu2b]
            tm = op("vector",
                    lambda eng: eng.memset(slot(3)[:, 0:512], 0.0),
                    after=[t_mu2b] + t_lncfree + dep0)
            for kt in range(NKT):
                ta = op("vector",
                        lambda eng, kt=kt, c0=c0: eng.tensor_tensor(
                            slot(2)[:, 0:512], xtile(kt, c0, 512),
                            slot(1)[:, 0:512], ALU.subtract),
                        after=[t_mu2b, tm] + t_lncfree)
                tm = op("vector",
                        lambda eng: eng.tensor_tensor(
                            slot(3)[:, 0:512], slot(3)[:, 0:512],
                            slot(2)[:, 0:512], ALU.max), after=[ta])
                tm = op("vector",
                        lambda eng: eng.scalar_tensor_tensor(
                            slot(3)[:, 0:512], slot(2)[:, 0:512], -1.0,
                            slot(3)[:, 0:512], ALU.mult, ALU.max),
                        after=[tm])
            tmx = tm
            for blk in range(4):
                tpm = op("tensor",
                         lambda eng, blk=blk: eng.transpose(
                             PS[5][:, 0:128],
                             slot(3)[:, blk * 128:(blk + 1) * 128],
                             IDF[:]),
                         after=[tmx, t_idf] + ps5_prev)
                tmx = op("vector",
                         lambda eng, blk=blk: eng.reduce_max(
                             colv[:, 60 + blk:61 + blk], PS[5][:, 0:128],
                             AX.X), after=[tpm])
                ps5_prev = [tmx]
            t_mxo = dma("sync",
                        rowb.ap()[0:1, 0:512].rearrange(
                            "a (m p) -> (a p) m", p=128).opt(),
                        colv[:, 60:64], after=[tmx], slow=True)
            t_mxi = dma("sync", srow(1, 512, 512), rowb[0:1, 0:512],
                        after=[t_mxo])
            t1 = op("vector",
                    lambda eng: eng.tensor_tensor(
                        srow(1, 512, 512), srow(1, 512, 512),
                        srow(0, 0, 512), ALU.mult), after=[t_mxi, t_r2])
            t2 = op("vector",
                    lambda eng: eng.tensor_single_scalar(
                        srow(1, 512, 512), srow(1, 512, 512), Q_EPS,
                        ALU.max), after=[t1])
            t3 = op("vector",
                    lambda eng: eng.reciprocal(
                        srow(2, 0, 512), srow(1, 512, 512)), after=[t2])
            t4 = op("vector",
                    lambda eng: eng.tensor_single_scalar(
                        srow(2, 0, 512), srow(2, 0, 512), 127.0, ALU.mult),
                    after=[t3])
            t5 = op("vector",
                    lambda eng: eng.tensor_single_scalar(
                        srow(2, 512, 512), srow(1, 512, 512), 1.0 / 127.0,
                        ALU.mult), after=[t4])
            tk1 = op("vector",
                     lambda eng: eng.tensor_scalar(
                         srow(3, 0, 512), srow(2, 512, 512),
                         smv[0:1, 16:17], None, ALU.mult), after=[t5])
            tk3 = op("vector",
                     lambda eng: eng.tensor_scalar(
                         srow(3, 512, 512), srow(2, 512, 512),
                         smv[0:1, 17:18], None, ALU.mult), after=[t5])
            trs = op("vector",
                     lambda eng: eng.tensor_tensor(
                         srow(0, 0, 512), srow(0, 0, 512), srow(2, 0, 512),
                         ALU.mult), after=[t4, tk1, tk3, t1])
            tc1 = op("vector",
                     lambda eng: eng.tensor_tensor(
                         scr[0:1, 0:512], scr[0:1, 0:512], srow(0, 0, 512),
                         ALU.mult), after=[trs, t_mu2b])
            tc2 = op("vector",
                     lambda eng: eng.tensor_scalar(
                         scr[0:1, 0:512], scr[0:1, 0:512], -1.0, 128.0,
                         ALU.mult, ALU.add), after=[tc1])
            t_rsb = bcast_row(srow(0, 0, 512), slot(1)[:, 0:512], 512,
                              5, [trs, t_mu2b] + ps5_prev)
            t_cb = bcast_row(scr[0:1, 0:512], slot(0)[:, 0:512], 512,
                             5, [tc2, t_rsb])
            ps5_prev = [t_cb]
            t_k1o = dma("sync", rowb[0:1, 0:512], srow(3, 0, 512),
                        after=[tk1])
            t_k1i = dma("sync", colv[:, 40:44],
                        rowb.ap()[0:1, 0:512].rearrange(
                            "a (m p) -> (a p) m", p=128).opt(),
                        after=[t_k1o], slow=True)
            t_k3o = dma("sync", rowb[0:1, 512:1024], srow(3, 512, 512),
                        after=[tk3, t_k1i])
            t_k3i = dma("sync", colv[:, 44:48],
                        rowb.ap()[0:1, 512:1024].rearrange(
                            "a (m p) -> (a p) m", p=128).opt(),
                        after=[t_k3o], slow=True)

            t_qn = []
            tq_prev = t_cb
            for kt in range(NKT):
                ta = op("vector",
                        lambda eng, kt=kt, c0=c0: eng.tensor_tensor(
                            scr[:, :], xtile(kt, c0, 512),
                            slot(1)[:, 0:512], ALU.mult),
                        after=[t_rsb, t_cb, tq_prev, t_wqdma2,
                               t_wqdma2b] + scr_prev)
                ta2 = op("vector",
                         lambda eng: eng.tensor_tensor(
                             scr[:, :], scr[:, :], slot(0)[:, 0:512],
                             ALU.add), after=[ta])
                tb = op("vector",
                        lambda eng: eng.tensor_copy(sci[:, :], scr[:, :]),
                        after=[ta2])
                td = op("vector",
                        lambda eng: eng.tensor_tensor(
                            scr[:, :], scr[:, :], sci[:, :], ALU.subtract),
                        after=[tb])
                te = op("vector",
                        lambda eng: eng.tensor_single_scalar(
                            scr[:, :], scr[:, :], 0.5, ALU.is_gt),
                        after=[td])
                tc_ = op("vector",
                         lambda eng, kt=kt: eng.scalar_tensor_tensor(
                             qn[:, kt * 512:(kt + 1) * 512], sci[:, :],
                             -128.0, scr[:, :], ALU.add, ALU.add),
                         after=[te])
                t_qn.append(tc_)
                tq_prev = tc_
                scr_prev = [tc_]

            t_hmax0 = []
            for mt in range(4):
                t_hmax0.append(op(
                    "vector",
                    lambda eng, mt=mt: eng.memset(
                        colv[:, 56 + mt:57 + mt], 0.0), after=dep0))
            t_a_mm, t_b_mm, t_h = {}, {}, {}
            th_prev = None
            for ch in range(NCHUNK):
                depA = [t_agw1] + ([t_a_mm[(ch - 1, 3)]] if ch else dep0)
                depB = [t_agw3] + ([t_b_mm[(ch - 1, 3)]] if ch else dep0)
                tldA = dma("sync",
                           sbuf_stream(0).rearrange("p (k c) -> p k c", k=16),
                           w1g.ap()[:, ch * CHUNK:(ch + 1) * CHUNK].rearrange(
                               "(k p) c -> p k c", p=128), after=depA)
                tldB = dma("scalar",
                           sbuf_stream(1).rearrange("p (k c) -> p k c", k=16),
                           w3g.ap()[:, ch * CHUNK:(ch + 1) * CHUNK].rearrange(
                               "(k p) c -> p k c", p=128), after=depB)
                for mt in range(4):
                    depm = t_qn + [tldA]
                    if ch >= 1:
                        depm.append(t_h[(ch - 1, mt)])
                    else:
                        depm = depm + t_lncfree + t_hrev
                    t_a_mm[(ch, mt)] = op(
                        "tensor",
                        lambda eng, mt=mt: [
                            eng.matmul(PS[mt][:, 0:CHUNK],
                                       qn[:, kt * 512 + mt * 128:
                                          kt * 512 + (mt + 1) * 128],
                                       sbuf_stream(0)[:, kt * CHUNK:
                                                      (kt + 1) * CHUNK],
                                       start=(kt == 0),
                                       stop=(kt == NKT - 1))
                            for kt in range(NKT)][-1],
                        after=depm)
                    depb = [tldB] + ([t_h[(ch - 1, mt)]] if ch else
                                     t_qn + t_lncfree)
                    t_b_mm[(ch, mt)] = op(
                        "tensor",
                        lambda eng, mt=mt: [
                            eng.matmul(PS[4 + mt][:, 0:CHUNK],
                                       qn[:, kt * 512 + mt * 128:
                                          kt * 512 + (mt + 1) * 128],
                                       sbuf_stream(1)[:, kt * CHUNK:
                                                      (kt + 1) * CHUNK],
                                       start=(kt == 0),
                                       stop=(kt == NKT - 1))
                            for kt in range(NKT)][-1],
                        after=depb)
                    tsa = op("scalar",
                             lambda eng, mt=mt: eng.activation(
                                 scr[:, 0:CHUNK], PS[mt][:, 0:CHUNK],
                                 AF.Silu, scale=colv[:, 40 + mt:41 + mt]),
                             after=[t_a_mm[(ch, mt)], t_k1i] +
                             ([th_prev] if th_prev else scr_prev))
                    tsb = op("scalar",
                             lambda eng, mt=mt: eng.activation(
                                 scr[:, CHUNK:2 * CHUNK],
                                 PS[4 + mt][:, 0:CHUNK], AF.Copy,
                                 scale=colv[:, 44 + mt:45 + mt]),
                             after=[t_b_mm[(ch, mt)], t_k3i])
                    th = op("vector",
                            lambda eng, mt=mt, ch=ch: eng.tensor_tensor(
                                raf(mt * HID + ch * CHUNK, CHUNK),
                                scr[:, 0:CHUNK], scr[:, CHUNK:2 * CHUNK],
                                ALU.mult), after=[tsa, tsb])
                    tr = op("vector",
                            lambda eng, mt=mt, ch=ch: eng.reduce_max(
                                smv[0:128, 24 + mt:25 + mt],
                                raf(mt * HID + ch * CHUNK, CHUNK), AX.X,
                                apply_absolute_value=True),
                            after=[th] + ([t_h[(ch - 1, mt)]] if ch else []))
                    t_h[(ch, mt)] = op(
                        "vector",
                        lambda eng, mt=mt: eng.tensor_tensor(
                            colv[:, 56 + mt:57 + mt],
                            colv[:, 56 + mt:57 + mt],
                            smv[0:128, 24 + mt:25 + mt], ALU.max),
                        after=[tr, t_hmax0[mt]])
                    th_prev = th
            scr_prev = [th_prev]
            t_h_all = [t_h[(NCHUNK - 1, mt)] for mt in range(4)]

            ts1 = op("vector",
                     lambda eng: eng.tensor_single_scalar(
                         colv[:, 48:52], colv[:, 56:60], Q_EPS, ALU.max),
                     after=t_h_all)
            ts2 = op("vector",
                     lambda eng: eng.reciprocal(
                         colv[:, 48:52], colv[:, 48:52]), after=[ts1])
            ts3 = op("vector",
                     lambda eng: eng.tensor_single_scalar(
                         colv[:, 48:52], colv[:, 48:52], 127.0, ALU.mult),
                     after=[ts2])
            tk2 = op("vector",
                     lambda eng: eng.reciprocal(
                         colv[:, 52:56], colv[:, 48:52]), after=[ts3])
            tk2b = op("vector",
                      lambda eng: eng.tensor_scalar(
                          colv[:, 52:56], colv[:, 52:56], smv[:, 18:19],
                          None, ALU.mult), after=[tk2])
            t_k2o = dma("sync",
                        rowb.ap()[0:1, 0:512].rearrange(
                            "a (m p) -> (a p) m", p=128).opt(),
                        colv[:, 52:56], after=[tk2b, t_k3i], slow=True)
            t_k2i = dma("sync", srow(2, 0, 512), rowb[0:1, 0:512],
                        after=[t_k2o, t5])
            t_k2b = bcast_row(srow(2, 0, 512), slot(2)[:, 0:512], 512,
                              5, [t_k2i] + ps5_prev + t_h_all)
            ps5_prev = [t_k2b]

            qh = ra[:, 0:11264].bitcast(BF16)
            t_qh = {}
            tqp = [ts3] + t_h_all + scr_prev
            for mt in range(4):
                for cq in range(11):
                    ta = op("vector",
                            lambda eng, mt=mt, cq=cq: eng.tensor_scalar(
                                scr[:, :], raf(mt * HID + cq * 512, 512),
                                colv[:, 48 + mt:49 + mt], 128.0,
                                ALU.mult, ALU.add), after=tqp)
                    tb = op("vector",
                            lambda eng: eng.tensor_copy(sci[:, :],
                                                        scr[:, :]),
                            after=[ta])
                    td = op("vector",
                            lambda eng: eng.tensor_tensor(
                                scr[:, :], scr[:, :], sci[:, :],
                                ALU.subtract), after=[tb])
                    te = op("vector",
                            lambda eng: eng.tensor_single_scalar(
                                scr[:, :], scr[:, :], 0.5, ALU.is_gt),
                            after=[td])
                    tf = op("vector",
                            lambda eng, mt=mt, cq=cq:
                            eng.scalar_tensor_tensor(
                                qh[:, mt * HID + cq * 512:
                                   mt * HID + (cq + 1) * 512],
                                sci[:, :], -128.0, scr[:, :],
                                ALU.add, ALU.add), after=[te])
                    t_qh[(mt, cq)] = tf
                    tqp = [tf]

            qhT = ra[:, 11264:22528].bitcast(BF16)

            def qhT_t(kt):
                return qhT[:, kt * 512:(kt + 1) * 512]

            tp_bank_free = {0: None, 1: None}
            t_tp_all = []
            t_qh_last = t_qh[(3, 10)]
            for kt in range(HKT):
                for mt in range(4):
                    bank = (kt * 4 + mt) % 2
                    dep = [t_qh_last, t_idb] + t_h_all
                    if tp_bank_free[bank] is not None:
                        dep.append(tp_bank_free[bank])
                    tmm = op(
                        "tensor",
                        lambda eng, kt=kt, mt=mt, bank=bank: eng.transpose(
                            PS[2 + bank][:, 0:512].bitcast(BF16)[:, 0:128],
                            qh[:, mt * HID + kt * 128:
                               mt * HID + (kt + 1) * 128],
                            IDB[:]),
                        after=dep)
                    tev = op("scalar",
                             lambda eng, kt=kt, mt=mt, bank=bank: eng.copy(
                                 qhT_t(kt)[:, mt * 128:(mt + 1) * 128],
                                 PS[2 + bank][:, 0:512].bitcast(
                                     BF16)[:, 0:128]),
                             after=[tmm])
                    tp_bank_free[bank] = tev
                    t_tp_all.append(tev)
            t_tp_last = [t_tp_all[-1]]

            t_g2ev = []
            g2_buf_free = {0: None, 1: None}
            for grp in range(2):
                last_mm = None
                for kt in range(HKT):
                    sb_i = kt % 2
                    depL = [t_agw2] + t_tp_last
                    if g2_buf_free[sb_i] is not None:
                        depL.append(g2_buf_free[sb_i])
                    tld = dma("sync", sbuf_stream(sb_i)[:, 0:1024],
                              w2g.ap()[kt * 128:(kt + 1) * 128,
                                       grp * 1024:(grp + 1) * 1024],
                              after=depL, semname="dL%d" % sb_i)
                    depM = [tld]
                    if kt == 0:
                        depM = depM + (t_g2ev[-8:] if grp == 1 else
                                       t_tp_last + t_lncfree + [t_k2b])
                    last_mm = op(
                        "tensor",
                        lambda eng, kt=kt, sb_i=sb_i: [
                            eng.matmul(PS[m8][:, :],
                                       sbuf_stream(sb_i)[:, m8 * 128:
                                                         (m8 + 1) * 128],
                                       qhT_t(kt),
                                       start=(kt == 0),
                                       stop=(kt == HKT - 1))
                            for m8 in range(8)][-1],
                        after=depM)
                    g2_buf_free[sb_i] = last_mm
                for m8 in range(8):
                    mtd = grp * 8 + m8
                    te1 = op("vector",
                             lambda eng, m8=m8: eng.tensor_tensor(
                                 scr[:, :], PS[m8][:, :],
                                 slot(2)[:, 0:512], ALU.mult),
                             after=[last_mm, t_k2b] +
                             (t_g2ev[-1:] if t_g2ev else scr_prev))
                    te2 = op("vector",
                             lambda eng, mtd=mtd, c0=c0: eng.tensor_tensor(
                                 xtile(mtd, c0, 512), scr[:, :],
                                 xtile(mtd, c0, 512), ALU.add),
                             after=[te1])
                    t_g2ev.append(te2)
                    t_out_ready[(hf, mtd)] = te2
            scr_prev = t_g2ev[-1:]
            t_half_done[hf] = t_g2ev[-1]

        OQ = [("sync", "dA"), ("scalar", "dO1"), ("gpsimd", "dO2")]
        for kt in range(NKT):
            q, sn = OQ[kt % 3]
            dma(q, outT[kt * 128:(kt + 1) * 128, :], xtile(kt),
                after=[t_out_ready[(0, kt)], t_out_ready[(1, kt)]],
                semname=sn)
        fin = cnt["dA"]
        f1 = cnt.get("dO1", 0)
        f2 = cnt.get("dO2", 0)
        steps["sync"].append(
            lambda eng, fin=fin, f1=f1, f2=f2: (
                eng.wait_ge(sem["dA"], fin),
                eng.wait_ge(sem["dO1"], f1),
                eng.wait_ge(sem["dO2"], f2)))

        with nc.Block() as block:
            @block.sync
            def _(eng):
                for fn in steps["sync"]:
                    fn(eng)

            @block.vector
            def _(eng):
                for fn in steps["vector"]:
                    fn(eng)

            @block.scalar
            def _(eng):
                for fn in steps["scalar"]:
                    fn(eng)

            @block.gpsimd
            def _(eng):
                for fn in steps["gpsimd"]:
                    fn(eng)

            @block.tensor
            def _(eng):
                for fn in steps["tensor"]:
                    fn(eng)

    return nc


def _chain_masks(core):
    m = np.zeros((1, 64), np.float32)
    b, s = core // 4, core % 4
    for st in range(3):
        j = b * 4 + st
        if st < s:
            m[0, st * 8 + j] = 1.0
        else:
            m[0, 24 + st] = 1.0
    for st in range(3):
        j = b * 4 + (3 - st)
        if (3 - st) > s:
            m[0, 27 + st * 8 + j] = 1.0
        else:
            m[0, 51 + st] = 1.0
    return np.repeat(m, 128, axis=0).astype(np.float32)


def kernel(L0_attn_x, L0_attn_W_in, L0_attn_W_fwd, L0_attn_W_bwd,
           L0_attn_W_out, L0_attn_gamma, L0_attn_beta, L0_ffn_w1,
           L0_ffn_w2, L0_ffn_w3):
    x = np.ascontiguousarray(np.asarray(L0_attn_x, dtype=np.float32))
    W_in = np.ascontiguousarray(np.asarray(L0_attn_W_in, np.float32))
    W_f = np.ascontiguousarray(np.asarray(L0_attn_W_fwd, np.float32))
    W_b = np.ascontiguousarray(np.asarray(L0_attn_W_bwd, np.float32))
    W_o = np.ascontiguousarray(np.asarray(L0_attn_W_out, np.float32))
    gam = np.asarray(L0_attn_gamma, np.float32).reshape(NKT, 128).T.copy()
    bet = np.asarray(L0_attn_beta, np.float32).reshape(NKT, 128).T.copy()
    w1T = np.ascontiguousarray(np.asarray(L0_ffn_w1, np.float32).T)
    w3T = np.ascontiguousarray(np.asarray(L0_ffn_w3, np.float32).T)
    w2T = np.ascontiguousarray(np.asarray(L0_ffn_w2, np.float32).T)
    ones = np.ones((128, 1), np.float32)
    identb = np.eye(128, dtype=ml_dtypes.bfloat16)
    identf = np.eye(128, dtype=np.float32)
    onesrow = np.ones((1, 128), np.float32)

    nc = build_nc()
    in_maps = []
    for core in range(8):
        b, s = core // 4, core % 4
        xs = np.ascontiguousarray(x[b, s * T:(s + 1) * T, :].T)
        in_maps.append({
            "xT": xs,
            "W_in": W_in, "W_f": W_f, "W_b": W_b, "W_o": W_o,
            "gamP": gam, "betP": bet,
            "w1s": np.ascontiguousarray(w1T[core * 256:(core + 1) * 256, :]),
            "w3s": np.ascontiguousarray(w3T[core * 256:(core + 1) * 256, :]),
            "w2s": np.ascontiguousarray(w2T[core * 704:(core + 1) * 704, :]),
            "cmask": _chain_masks(core),
            "ones": ones,
            "identb": identb,
            "identf": identf,
            "onesrow": onesrow,
        })
    res = run_bass_kernel_spmd(nc, in_maps, core_ids=list(range(8)))
    out = np.empty((B, L, D), np.float32)
    for core in range(8):
        b, s = core // 4, core % 4
        out[b, s * T:(s + 1) * T, :] = res.results[core]["outT"].T
    return out
